# version 01
# baseline (speedup 1.0000x reference)
import numpy as np
from contextlib import ExitStack

import concourse.bass as bass
import concourse.mybir as mybir
from concourse.bass_utils import run_bass_kernel_spmd

F32 = mybir.dt.float32
BF16 = mybir.dt.bfloat16
AF = mybir.ActivationFunctionType
ALU = mybir.AluOpType
AX = mybir.AxisListType

P = 128
T = 2048
NCORES = 8
D = 1024
DCH = D // P
NTT = T // P

ENGINES = ("sync", "tensor", "vector", "scalar", "gpsimd")


class Builder:

    def __init__(self, nc):
        self.nc = nc
        self.stages = []
        self.ctx = ExitStack()
        self._nametag = 0

    def sb(self, name, shape, dtype):
        return self.ctx.enter_context(self.nc.sbuf_tensor(name, list(shape), dtype))

    def ps(self, name, shape, dtype=F32):
        return self.ctx.enter_context(self.nc.psum_tensor(name, list(shape), dtype))

    def sem(self, name):
        return self.ctx.enter_context(self.nc.semaphore(name))

    def stage(self, label=""):
        s = {e: [] for e in ENGINES}
        s["label"] = label
        self.stages.append(s)
        return s

    def on(self, engine, fn):
        self.stages[-1][engine].append(fn)

    def dma(self, engine, out, in_, defer=False):
        def f(eng, _out=out, _in=in_, _defer=defer):
            eng.dma_start(out=_out, in_=_in).then_inc(self._dmasem[eng._ename], 16)
            self._dmacnt[eng._ename] += 16
            if _defer:
                self._dmadefer[eng._ename] += 16
        self.on(engine, f)

    def dma_fence(self, engine):
        def f(eng):
            eng.wait_ge(self._dmasem[eng._ename], self._dmacnt[eng._ename])
            self._dmadefer[eng._ename] = 0
        self.on(engine, f)

    def collective(self, kind, op, groups, ins, outs):
        csem = self.sem(f"cc{self._nametag}")
        self._nametag += 1

        def f(eng):
            eng.collective_compute(kind, op, replica_groups=groups,
                                   ins=ins, outs=outs).then_inc(csem)
            eng.wait_ge(csem, 1)
        self.on("gpsimd", f)

    def finalize(self):
        nc = self.nc
        bar = self.sem("bar")
        self._dmasem = {e: self.sem(f"dma_{e}") for e in ENGINES}
        self._dmacnt = {e: 0 for e in ENGINES}
        self._dmadefer = {e: 0 for e in ENGINES}
        builder = self
        with nc.Block() as block:
            def make(ename):
                def emit(eng):
                    eng._ename = ename
                    for si, stg in enumerate(builder.stages):
                        if si > 0:
                            eng.wait_ge(bar, len(ENGINES) * si)
                        before = builder._dmacnt[ename]
                        pre_defer = builder._dmadefer[ename]
                        for fn in stg[ename]:
                            fn(eng)
                        after = builder._dmacnt[ename]
                        if after - builder._dmadefer[ename] > before - pre_defer:
                            eng.wait_ge(builder._dmasem[ename],
                                        after - builder._dmadefer[ename])
                        eng.drain()
                        eng.nop().then_inc(bar, 1)
                return emit
            for ename in ENGINES:
                getattr(block, ename)(make(ename))
        self.ctx.close()
        return nc


def _din(nc, name, shape, dtype=F32):
    return nc.dram_tensor(name, list(shape), dtype, kind="ExternalInput").ap()


def _dout(nc, name, shape, dtype=F32):
    return nc.dram_tensor(name, list(shape), dtype, kind="ExternalOutput").ap()



def ln_tmp(b, tag, dfree=D):
    t = {k: b.sb(f"ln_{tag}_{k}", [P, 1], F32)
         for k in ("m", "ss", "var", "std", "rstd", "m2")}
    t["sq"] = b.sb(f"ln_{tag}_sq", [P, dfree], F32)
    return t

def emit_ln(b, x_ap, out_ap, tmp, dfree, eps=1e-5, gb=None):
    b.stage("ln_a")
    def v1(eng):
        eng.reduce_sum(tmp["m"][:], x_ap, axis=AX.X)
        eng.drain()
        eng.tensor_scalar_mul(tmp["m"][:], tmp["m"][:], 1.0 / dfree)
    b.on("vector", v1)

    def s1(eng):
        eng.activation(out=tmp["sq"][:, 0:dfree], in_=x_ap, func=AF.Square,
                       accum_out=tmp["ss"][:])
    b.on("scalar", s1)

    b.stage("ln_b")
    def v2(eng):
        eng.tensor_scalar_mul(tmp["var"][:], tmp["ss"][:], 1.0 / dfree)
        eng.tensor_mul(tmp["m2"][:], tmp["m"][:], tmp["m"][:])
        eng.drain()
        eng.tensor_sub(tmp["var"][:], tmp["var"][:], tmp["m2"][:])
        eng.drain()
        eng.tensor_scalar_add(tmp["var"][:], tmp["var"][:], eps)
    b.on("vector", v2)

    b.stage("ln_c")
    def s2(eng):
        eng.activation(out=tmp["std"][:], in_=tmp["var"][:], func=AF.Sqrt)
    b.on("scalar", s2)

    b.stage("ln_d")
    def v3(eng):
        eng.reciprocal(tmp["rstd"][:], tmp["std"][:])
        eng.drain()
        eng.tensor_scalar(out=out_ap, in0=x_ap, scalar1=tmp["m"][:],
                          scalar2=tmp["rstd"][:], op0=ALU.subtract, op1=ALU.mult)
        if gb is not None:
            eng.scalar_tensor_tensor(out=out_ap, in0=out_ap, scalar=1.0,
                                     in1=gb[0], op0=ALU.mult, op1=ALU.mult)
            eng.tensor_add(out_ap, out_ap, gb[1])
    b.on("vector", v3)


def emit_bcast_row(b, psum_ap, ones_col_ap, row_ap):
    def f(eng):
        eng.matmul(psum_ap, ones_col_ap, row_ap, start=True, stop=True)
    b.on("tensor", f)


D_SSM = 2048
CONV_DIM = 2304
NCCH = CONV_DIM // P
D_IN_PROJ = 4384
NHEADS = 32
HEADDIM = 64
D_STATE = 128
HALO1 = 128
TH1 = HALO1 + T
NT1 = TH1 // P


def build_launch1(upto=99):
    nc = bass.Bass("TRN2", target_bir_lowering=False, detect_race_conditions=False)

    x_in = _din(nc, "x", [TH1, D])
    w_in = _din(nc, "w_in", [D, D_IN_PROJ])
    convw = _din(nc, "convw_t", [CONV_DIM, 4])
    convb = _din(nc, "convb", [CONV_DIM, 1])
    dtb = _din(nc, "dt_bias", [1, NHEADS])
    alog = _din(nc, "a_log", [1, NHEADS])
    onesc = _din(nc, "ones_col", [1, P])
    ident_in = _din(nc, "ident_bf", [P, P], BF16)

    xn_out = _dout(nc, "xn", [T, D])
    z_out = _dout(nc, "z_tm", [T, D_SSM])
    delta_out = _dout(nc, "delta", [T, NHEADS])
    da_out = _dout(nc, "dA", [T, NHEADS])
    xconv_out = _dout(nc, "xconv", [CONV_DIM, T])

    b = Builder(nc)
    xt = [b.sb(f"xt{i}", [P, D], F32) for i in range(NT1)]
    xn_f = b.sb("xn_f", [P, D], F32)
    xn_bf = b.sb("xn_bf", [P, D], BF16)
    xnfm = [b.sb(f"xnfm{c}", [P, TH1], BF16) for c in range(DCH)]
    wbf = [b.sb(f"wbf{c}", [P, D_IN_PROJ], BF16) for c in range(DCH)]
    ident_bf = b.sb("identbf", [P, P], BF16)
    cw = b.sb("cw", [P, 4 * NCCH], F32)
    cb = b.sb("cb", [P, NCCH], F32)
    xbc = b.sb("xbc", [P, TH1], F32)
    xcv = b.sb("xcv", [P, T], F32)
    ztile = b.sb("ztile", [P, 512], F32)
    dtile = b.sb("dtile", [P, NHEADS], F32)
    datile = b.sb("datile", [P, NHEADS], F32)
    sftmp = b.sb("sftmp", [P, NHEADS], F32)
    sftmp2 = b.sb("sftmp2", [P, NHEADS], F32)
    dtbbc = b.sb("dtbbc", [P, NHEADS], F32)
    anegbc = b.sb("anegbc", [P, NHEADS], F32)
    ones_sb = b.sb("ones_sb", [1, P], F32)
    dtb_row = b.sb("dtb_row", [1, NHEADS], F32)
    al_row = b.sb("al_row", [1, NHEADS], F32)
    tmp = ln_tmp(b, "l1")

    pt = b.ps("pt", [P, 512], BF16)
    pt2 = b.ps("pt2", [P, 512], BF16)
    pz = b.ps("pz", [P, 512], F32)
    pbc = b.ps("pbc", [P, NHEADS], F32)
    px1 = b.ps("px1", [P, 512], F32)
    px2 = b.ps("px2", [P, 512], F32)

    b.stage("load")
    for i in range(NT1):
        b.dma("sync", xt[i][:], x_in[i * P:(i + 1) * P, :])
    for c in range(DCH):
        b.dma("gpsimd", wbf[c][:], w_in[c * P:(c + 1) * P, :])
    b.dma("sync", ident_bf[:], ident_in)
    for ch in range(NCCH):
        b.dma("sync", cw[:, ch * 4:(ch + 1) * 4], convw[ch * P:(ch + 1) * P, :])
        b.dma("sync", cb[:, ch:ch + 1], convb[ch * P:(ch + 1) * P, :])
    b.dma("sync", ones_sb[:], onesc)
    b.dma("sync", dtb_row[:], dtb)
    b.dma("sync", al_row[:], alog)

    b.stage("bc1")
    emit_bcast_row(b, pbc[:, :], ones_sb[:], dtb_row[:])
    b.stage("bc2")
    b.on("vector", lambda eng: eng.tensor_copy(dtbbc[:], pbc[:]))
    b.stage("bc3")
    emit_bcast_row(b, pbc[:, :], ones_sb[:], al_row[:])
    b.stage("bc4")
    b.on("scalar", lambda eng: eng.activation(out=anegbc[:], in_=pbc[:], func=AF.Exp))
    b.stage("bc5")
    b.on("vector", lambda eng: eng.tensor_scalar_mul(anegbc[:], anegbc[:], -1.0))

    for i in range(NT1 if upto >= 1 else 0):
        b.stage(f"ln{i}")
        emit_ln(b, xt[i][:], xn_f[:], tmp, D)
        b.stage(f"cast{i}")
        b.on("vector", lambda eng: eng.tensor_copy(xn_bf[:], xn_f[:]))
        if i > 0:
            b.dma("sync", xn_out[(i - 1) * P:i * P, :], xn_f[:])
        b.stage(f"tra{i}")
        for dc in range(DCH):
            pp = pt if dc < 4 else pt2
            def tr(eng, dc=dc, pp=pp):
                eng.transpose(pp[:, (dc % 4) * 128:(dc % 4 + 1) * 128],
                              xn_bf[:, dc * 128:(dc + 1) * 128], ident_bf[:])
            b.on("tensor", tr)
        b.stage(f"trb{i}")
        for dc in range(DCH):
            pp = pt if dc < 4 else pt2
            def ev(eng, i=i, dc=dc, pp=pp):
                eng.tensor_copy(xnfm[dc][:, i * P:(i + 1) * P],
                                pp[:, (dc % 4) * 128:(dc % 4 + 1) * 128])
            b.on("vector", ev)

    for ch in range(NCCH if upto >= 2 else 0):
        wcol0 = D_SSM + ch * P
        mgw = [512, 512, 512, 512, 128]
        for mg in range(6):
            b.stage(f"xbc{ch}_{mg}")
            if mg < 5:
                pp = px1 if mg % 2 == 0 else px2
                w = mgw[mg]
                for kc in range(DCH):
                    def mm(eng, wcol0=wcol0, mg=mg, kc=kc, w=w, pp=pp):
                        eng.matmul(pp[:, 0:w], wbf[kc][:, wcol0:wcol0 + P],
                                   xnfm[kc][:, mg * 512:mg * 512 + w],
                                   start=(kc == 0), stop=(kc == DCH - 1))
                    b.on("tensor", mm)
            if mg > 0:
                pq = px1 if (mg - 1) % 2 == 0 else px2
                wv = mgw[mg - 1]
                def ev(eng, mg=mg, wv=wv, pq=pq):
                    eng.tensor_copy(xbc[:, (mg - 1) * 512:(mg - 1) * 512 + wv], pq[:, 0:wv])
                b.on("vector", ev)
        b.stage(f"conv{ch}")
        def cv(eng, ch=ch):
            eng.tensor_scalar(out=xcv[:], in0=xbc[:, 125:125 + T],
                              scalar1=cw[:, ch * 4:ch * 4 + 1],
                              scalar2=cb[:, ch:ch + 1], op0=ALU.mult, op1=ALU.add)
            for k in range(1, 4):
                eng.drain()
                eng.scalar_tensor_tensor(out=xcv[:], in0=xbc[:, 125 + k:125 + k + T],
                                         scalar=cw[:, ch * 4 + k:ch * 4 + k + 1],
                                         in1=xcv[:], op0=ALU.mult, op1=ALU.add)
        b.on("vector", cv)
        b.stage(f"silu{ch}")
        b.on("scalar", lambda eng: eng.activation(out=xcv[:], in_=xcv[:], func=AF.Silu))
        b.stage(f"st{ch}")
        b.dma("sync", xconv_out[ch * P:(ch + 1) * P, :], xcv[:])

    zt2 = [ztile, b.sb("ztile2", [P, 512], F32)]
    for i in range(NTT if upto >= 3 else 0):
        col = HALO1 + i * P
        for ng in range(5):
            b.stage(f"z{i}_{ng}")
            if ng < 4:
                pp = px1 if ng % 2 == 0 else px2
                for kc in range(DCH):
                    def mm(eng, i=i, ng=ng, kc=kc, pp=pp, col=col):
                        eng.matmul(pp[:, :], xnfm[kc][:, col:col + P],
                                   wbf[kc][:, ng * 512:(ng + 1) * 512],
                                   start=(kc == 0), stop=(kc == DCH - 1))
                    b.on("tensor", mm)
            if ng > 0:
                pq = px1 if (ng - 1) % 2 == 0 else px2
                zb = zt2[(ng - 1) % 2]
                def ev(eng, pq=pq, zb=zb):
                    eng.tensor_copy(zb[:, :], pq[:, :])
                b.on("vector", ev)
            if ng > 1:
                zb2 = zt2[(ng - 2) % 2]
                b.dma("sync", z_out[i * P:(i + 1) * P, (ng - 2) * 512:(ng - 1) * 512], zb2[:])
        b.stage(f"zst{i}")
        b.dma("sync", z_out[i * P:(i + 1) * P, 3 * 512:4 * 512], zt2[1][:])
        for kc in range(DCH):
            def mm(eng, i=i, kc=kc, col=col):
                eng.matmul(pbc[:, :], xnfm[kc][:, col:col + P],
                           wbf[kc][:, 4352:4384],
                           start=(kc == 0), stop=(kc == DCH - 1))
            b.on("tensor", mm)
        b.stage(f"dtu{i}")
        def ev(eng):
            eng.tensor_add(dtile[:], pbc[:], dtbbc[:])
            eng.drain()
            eng.tensor_scalar_mul(sftmp[:], dtile[:], -1.0)
            eng.drain()
            eng.tensor_max(sftmp2[:], dtile[:], sftmp[:])
            eng.drain()
            eng.tensor_scalar_mul(sftmp2[:], sftmp2[:], -1.0)
        b.on("vector", ev)
        b.stage(f"dts1_{i}")
        b.on("scalar", lambda eng: eng.activation(out=sftmp[:], in_=sftmp2[:], func=AF.Exp))
        b.stage(f"dts2_{i}")
        b.on("scalar", lambda eng: eng.activation(out=sftmp[:], in_=sftmp[:], func=AF.Ln, bias=1.0))
        b.stage(f"dtf{i}")
        def fin(eng):
            eng.tensor_scalar_max(sftmp2[:], dtile[:], 0.0)
            eng.drain()
            eng.tensor_add(dtile[:], sftmp2[:], sftmp[:])
            eng.drain()
            eng.tensor_mul(datile[:], dtile[:], anegbc[:])
        b.on("vector", fin)
        b.stage(f"dtst{i}")
        b.dma("sync", delta_out[i * P:(i + 1) * P, :], dtile[:])
        b.dma("sync", da_out[i * P:(i + 1) * P, :], datile[:])

    b.finalize()
    return nc


if __name__ == "__main__":
    nc = build_launch1()
    print("launch1 built")



def _f32(a):
    return np.ascontiguousarray(a, dtype=np.float32)


def host_l1_inmaps(inp):
    import ml_dtypes
    x = np.asarray(inp["L0_attn_hidden_states"])
    w_in = _f32(inp["L0_attn_W_in"])
    convw_t = _f32(np.asarray(inp["L0_attn_conv_w"]).T)
    convb = _f32(np.asarray(inp["L0_attn_conv_b"]).reshape(CONV_DIM, 1))
    dtb = _f32(np.asarray(inp["L0_attn_dt_bias"]).reshape(1, NHEADS))
    alog = _f32(np.asarray(inp["L0_attn_A_log"]).reshape(1, NHEADS))
    ones_col = np.ones((1, P), dtype=np.float32)
    ident = np.eye(P, dtype=ml_dtypes.bfloat16)
    maps = []
    for c in range(NCORES):
        bb, hh = c // 2, c % 2
        r0 = hh * T
        xs = np.zeros((TH1, D), dtype=np.float32)
        lo = max(0, r0 - HALO1)
        xs[HALO1 - (r0 - lo):] = x[bb, lo:r0 + T]
        maps.append({"x": xs, "w_in": w_in, "convw_t": convw_t, "convb": convb,
                     "dt_bias": dtb, "a_log": alog, "ones_col": ones_col,
                     "ident_bf": ident})
    return maps


def run_launch1(inp):
    nc = build_launch1()
    res = run_bass_kernel_spmd(nc, host_l1_inmaps(inp), core_ids=list(range(NCORES)))
    return res.results


NCH = 16
HHALF = 16


def build_launch2():
    nc = bass.Bass("TRN2", target_bir_lowering=False, detect_race_conditions=False)

    xconv_in = _din(nc, "xconv", [CONV_DIM, T])
    delta_in = _din(nc, "delta", [T, NHEADS])
    da_in = _din(nc, "dA", [T, NHEADS])
    drep_in = _din(nc, "d_rep", [1, D_SSM])
    trimask_in = _din(nc, "trimask", [P, P])
    onesc = _din(nc, "ones_col", [1, P])
    ones32_in = _din(nc, "ones32", [32, P])
    identb_in = _din(nc, "ident_bf", [P, P], BF16)
    identf_in = _din(nc, "ident_f", [P, P], F32)

    y_out = _dout(nc, "ylocal", [T, D_SSM])
    send_out = _dout(nc, "send", [P, D_SSM])
    atot_out = _dout(nc, "atot", [NCH, NHEADS])
    lam_out = _dout(nc, "lam", [T, NHEADS])

    lam_dram = nc.dram_tensor("lam_dram", [NHEADS, T], F32, kind="Internal")
    a2_dram = nc.dram_tensor("a2_dram", [NCH, NHEADS], F32, kind="Internal")

    b = Builder(nc)
    xblk = [b.sb(f"xblk{i}", [P, D_SSM], BF16) for i in range(2)]
    xtm_c = b.sb("xtm_c", [P, D_SSM], BF16)
    bfm = b.sb("bfm", [P, T], BF16)
    cfm = b.sb("cfm", [P, T], BF16)
    srun = b.sb("srun", [P, D_SSM], F32)
    srun_bf = b.sb("srun_bf", [P, D_SSM], BF16)
    esb = b.sb("esb", [P, HHALF * P], F32)
    ptil = b.sb("ptil", [P, HHALF * P], BF16)
    gm = b.sb("gm", [P, P], F32)
    btm = b.sb("btm", [P, P], BF16)
    bpp = b.sb("bpp", [P, HHALF * P], BF16)
    wv = b.sb("wv", [P, NHEADS], F32)
    dbc = b.sb("dbc", [P, D_SSM], F32)
    elam = b.sb("elam", [P, NHEADS], F32)
    ysb = b.sb("ysb", [P, D_SSM], F32)
    tscr = b.sb("tscr", [P, 1024], F32)
    tscr2 = b.sb("tscr2", [P, 1024], F32)
    dl_c = b.sb("dl_c", [P, NHEADS], F32)
    da_c = b.sb("da_c", [P, NHEADS], F32)
    lam_tm = b.sb("lam_tm", [P, NHEADS], F32)
    daT = b.sb("daT", [32, P], F32)
    lamT = b.sb("lamT", [32, T], F32)
    ltot = b.sb("ltot", [32, NCH], F32)
    ones32 = b.sb("o32", [32, P], F32)
    aexp = b.sb("aexp", [32, NCH + 1], F32)
    at_sb = b.sb("at_sb", [NCH, 32], F32)
    lrow = [b.sb(f"lrow{i}", [1, NHEADS * P], F32) for i in range(2)]
    arow = [b.sb(f"arow{i}", [1, NHEADS], F32) for i in range(2)]
    abc = b.sb("abc", [P, NHEADS], F32)
    trimask = b.sb("trim", [P, P], F32)
    identb = b.sb("idb", [P, P], BF16)
    identf = b.sb("idf", [P, P], F32)
    ones_sb = b.sb("onesc", [1, P], F32)
    drep_row = b.sb("drep_row", [1, D_SSM], F32)

    pbig = b.ps("pbig", [P, 2048], F32)
    pA = b.ps("pA", [P, 1024], F32)
    pB = b.ps("pB", [P, 1024], F32)
    pAb = pA[:].bitcast(BF16)
    pBb = pB[:].bitcast(BF16)

    b.stage("load")
    for dc in range(16):
        b.dma("gpsimd", xblk[0][:, dc * P:(dc + 1) * P],
              xconv_in[dc * P:(dc + 1) * P, 0:P], defer=True)
    b.dma("gpsimd", bfm[:], xconv_in[D_SSM:D_SSM + P, :])
    b.dma("gpsimd", cfm[:], xconv_in[D_SSM + P:D_SSM + 2 * P, :])
    b.dma("sync", trimask[:], trimask_in)
    b.dma("sync", identb[:], identb_in)
    b.dma("sync", identf[:], identf_in)
    b.dma("sync", ones_sb[:], ones32_in[0:1, :])
    b.dma("sync", ones32[:], ones32_in)
    b.dma("sync", drep_row[:], drep_in)

    for g in range(4):
        b.stage(f"dbc{g}")
        emit_bcast_row(b, pA[:, 512:1024], ones_sb[:], drep_row[0:1, g * 512:(g + 1) * 512])
        b.stage(f"dbce{g}")
        b.on("vector", lambda eng, g=g: eng.tensor_copy(dbc[:, g * 512:(g + 1) * 512],
                                                        pA[:, 512:1024]))
    b.stage("sinit")
    def _si(eng):
        eng.memset(srun[:], 0.0)
        eng.memset(srun_bf[:], 0.0)
    b.on("vector", _si)

    for c in range(NCH):
        b.stage(f"lamld{c}")
        b.dma("sync", da_c[:], da_in[c * P:(c + 1) * P, :])
        b.stage(f"lamtr{c}")
        b.on("tensor", lambda eng: eng.transpose(pB[0:32, 0:128], da_c[:, :], identf[:]))
        b.stage(f"lamcp{c}")
        b.on("vector", lambda eng: eng.tensor_copy(daT[:, :], pB[0:32, 0:128]))
        b.stage(f"lamsc{c}")
        def sc(eng, c=c):
            eng.tensor_tensor_scan(out=lamT[:, c * P:(c + 1) * P], data0=ones32[:, :],
                                   data1=daT[:, :], initial=0.0,
                                   op0=ALU.mult, op1=ALU.add)
            eng.drain()
            eng.tensor_copy(ltot[:, c:c + 1], lamT[:, c * P + 127:c * P + 128])
        b.on("vector", sc)
        b.stage(f"lamtm{c}")
        b.on("tensor", lambda eng, c=c: eng.transpose(pB[:, 0:32],
                                                      lamT[:, c * P:(c + 1) * P],
                                                      identf[0:32, 0:32]))
        b.stage(f"lamtme{c}")
        b.on("vector", lambda eng: eng.tensor_copy(lam_tm[:, :], pB[:, 0:32]))
        b.dma("sync", lam_out[c * P:(c + 1) * P, :], lam_tm[:])
    b.stage("acum")
    def ac(eng):
        eng.memset(aexp[:, 0:1], 0.0)
        eng.tensor_tensor_scan(out=aexp[:, 1:NCH + 1], data0=ones32[:, 0:NCH],
                               data1=ltot[:, :], initial=0.0, op0=ALU.mult, op1=ALU.add)
    b.on("vector", ac)
    b.stage("aexp")
    b.on("scalar", lambda eng: eng.activation(out=aexp[:], in_=aexp[:], func=AF.Exp))
    b.stage("atr")
    b.on("tensor", lambda eng: eng.transpose(pB[0:NCH, 0:32], aexp[:, 0:NCH], identf[0:32, 0:32]))
    b.stage("atre")
    b.on("vector", lambda eng: eng.tensor_copy(at_sb[:, :], pB[0:NCH, 0:32]))
    b.stage("ast")
    b.dma("sync", atot_out, at_sb[:])
    b.dma("sync", lam_dram.ap(), lamT[:])

    b.stage("ach")
    b.on("scalar", lambda eng: eng.activation(out=aexp[:, 0:NCH], in_=ltot[:, :], func=AF.Exp))
    b.stage("achtr")
    b.on("tensor", lambda eng: eng.transpose(pB[0:NCH, 0:32], aexp[:, 0:NCH], identf[0:32, 0:32]))
    b.stage("achtre")
    b.on("vector", lambda eng: eng.tensor_copy(at_sb[:, :], pB[0:NCH, 0:32]))
    b.stage("achst")
    b.dma("sync", a2_dram.ap(), at_sb[:])

    b.stage("lrow0")
    b.dma("sync", lrow[0][:].rearrange("o (h t) -> o h t", t=P), lam_dram.ap()[:, 0:P].unsqueeze(0))
    b.dma("sync", arow[0][:], a2_dram.ap()[0:1, :])

    for c in range(NCH):
        lr = lrow[c % 2]
        ar = arow[c % 2]
        b.stage(f"pre{c}")
        b.dma("sync", dl_c[:], delta_in[c * P:(c + 1) * P, :])
        b.dma("sync", lam_tm[:], lam_out[c * P:(c + 1) * P, :])
        if c + 1 < NCH:
            b.dma("sync", lrow[(c + 1) % 2][:].rearrange("o (h t) -> o h t", t=P),
                  lam_dram.ap()[:, (c + 1) * P:(c + 2) * P].unsqueeze(0))
            b.dma("sync", arow[(c + 1) % 2][:], a2_dram.ap()[c + 1:c + 2, :])
        b.dma_fence("gpsimd")
        if c + 1 < NCH:
            for dc in range(16):
                b.dma("gpsimd", xblk[(c + 1) % 2][:, dc * P:(dc + 1) * P],
                      xconv_in[dc * P:(dc + 1) * P, (c + 1) * P:(c + 2) * P], defer=True)
        xb = xblk[c % 2]
        def gmm(eng, c=c, xb=xb):
            eng.matmul(pA[:, 0:128], bfm[:, c * P:(c + 1) * P],
                       cfm[:, c * P:(c + 1) * P], start=True, stop=True)
            eng.transpose(pAb[:, 1024:1152], bfm[:, c * P:(c + 1) * P], identb[:])
            for dc in range(8):
                eng.transpose(pAb[:, 1280 + dc * 128:1280 + (dc + 1) * 128] if False
                              else pBb[:, dc * 128:(dc + 1) * 128],
                              xb[:, dc * P:(dc + 1) * P], identb[:])
        b.on("tensor", gmm)
        b.stage(f"pre2{c}")
        def gev(eng, xb=xb):
            eng.tensor_mul(gm[:, :], pA[:, 0:128], trimask[:, :])
            eng.tensor_copy(btm[:, :], pAb[:, 1024:1152])
            for dc in range(8):
                eng.tensor_copy(xtm_c[:, dc * 128:(dc + 1) * 128],
                                pBb[:, dc * 128:(dc + 1) * 128])
        b.on("vector", gev)
        def gmm2(eng, xb=xb):
            for dc in range(8, 16):
                eng.transpose(pBb[:, (dc - 8) * 128:(dc - 7) * 128],
                              xb[:, dc * P:(dc + 1) * P], identb[:])
        b.on("tensor", None) if False else None
        b.stage(f"pre3{c}")
        b.on("tensor", gmm2)
        b.stage(f"pre4{c}")
        def gev2(eng, xb=xb):
            for dc in range(8, 16):
                eng.tensor_copy(xtm_c[:, dc * 128:(dc + 1) * 128],
                                pBb[:, (dc - 8) * 128:(dc - 7) * 128])
        b.on("vector", gev2)
        b.on("scalar", lambda eng: eng.activation(out=elam[:], in_=lam_tm[:], func=AF.Exp))
        b.stage(f"abc{c}")
        emit_bcast_row(b, pB[:, 0:32], ones_sb[:], ar[0:1, :])
        b.stage(f"abce{c}")
        b.on("vector", lambda eng: eng.tensor_copy(abc[:, :], pB[:, 0:32]))

        for hf in range(2):
            H = hf * HHALF
            b.stage(f"sA{c}_{hf}")
            for j in range(HHALF):
                def mm(eng, j=j, H=H, lr=lr):
                    eng.matmul(pbig[:, j * 128:(j + 1) * 128], ones_sb[:],
                               lr[0:1, (H + j) * P:(H + j + 1) * P],
                               start=True, stop=True)
                b.on("tensor", mm)
            b.stage(f"sB1{c}_{hf}")
            def bb1(eng, H=H):
                for j in range(HHALF):
                    eng.tensor_scalar(out=esb[:, j * 128:(j + 1) * 128],
                                      in0=pbig[:, j * 128:(j + 1) * 128],
                                      scalar1=lam_tm[:, H + j:H + j + 1], scalar2=0.0,
                                      op0=ALU.subtract, op1=ALU.min)
            b.on("vector", bb1)
            b.stage(f"sB2{c}_{hf}")
            b.on("scalar", lambda eng: eng.activation(out=esb[:], in_=esb[:], func=AF.Exp))
            b.stage(f"sC{c}_{hf}")
            def cc3(eng, H=H):
                for j in range(HHALF):
                    eng.tensor_mul(wv[:, H + j:H + j + 1], dl_c[:, H + j:H + j + 1],
                                   esb[:, j * 128 + 127:j * 128 + 128])
                eng.drain()
                for j in range(HHALF):
                    eng.scalar_tensor_tensor(out=ptil[:, j * 128:(j + 1) * 128],
                                             in0=esb[:, j * 128:(j + 1) * 128],
                                             scalar=dl_c[:, H + j:H + j + 1],
                                             in1=gm[:, :], op0=ALU.mult, op1=ALU.mult)
                eng.drain()
                for j in range(HHALF):
                    eng.tensor_scalar_mul(bpp[:, j * 128:(j + 1) * 128], btm[:, :],
                                          wv[:, H + j:H + j + 1])
            b.on("vector", cc3)
            b.stage(f"sD{c}_{hf}")
            def dd(eng, c=c, H=H):
                for j in range(HHALF):
                    eng.matmul(pA[:, j * 64:(j + 1) * 64],
                               ptil[:, j * 128:(j + 1) * 128],
                               xtm_c[:, (H + j) * 64:(H + j + 1) * 64],
                               start=True, stop=True)
                eng.matmul(pB[:, 0:512], cfm[:, c * P:(c + 1) * P],
                           srun_bf[:, H * 64:H * 64 + 512], start=True, stop=True)
                eng.matmul(pB[:, 512:1024], cfm[:, c * P:(c + 1) * P],
                           srun_bf[:, H * 64 + 512:H * 64 + 1024], start=True, stop=True)
                for j in range(HHALF):
                    eng.matmul(pbig[:, j * 64:(j + 1) * 64],
                               bpp[:, j * 128:(j + 1) * 128],
                               xtm_c[:, (H + j) * 64:(H + j + 1) * 64],
                               start=True, stop=True)
            b.on("tensor", dd)
            b.stage(f"sE{c}_{hf}")
            def ee2(eng, c=c, H=H):
                el3 = elam[:, H:H + HHALF].unsqueeze(2).broadcast_to((P, HHALF, 64))
                eng.tensor_tensor(out=tscr[:].rearrange("p (h q) -> p h q", q=64),
                                  in0=pB[:].rearrange("p (h q) -> p h q", q=64),
                                  in1=el3, op=ALU.mult)
                eng.tensor_mul(tscr2[:], xtm_c[:, H * 64:(H + HHALF) * 64],
                               dbc[:, H * 64:(H + HHALF) * 64])
                eng.drain()
                eng.tensor_add(tscr[:], tscr[:], tscr2[:])
                eng.drain()
                eng.tensor_add(ysb[:, H * 64:(H + HHALF) * 64], tscr[:], pA[:])
            b.on("vector", ee2)
            b.stage(f"sF{c}_{hf}")
            def ff(eng, H=H):
                a3 = abc[:, H:H + HHALF].unsqueeze(2).broadcast_to((P, HHALF, 64))
                eng.tensor_tensor(out=tscr[:].rearrange("p (h q) -> p h q", q=64),
                                  in0=srun[:, H * 64:(H + HHALF) * 64].rearrange(
                                      "p (h q) -> p h q", q=64),
                                  in1=a3, op=ALU.mult)
                eng.drain()
                eng.tensor_add(srun[:, H * 64:(H + HHALF) * 64], tscr[:], pbig[:, 0:1024])
                eng.drain()
                eng.tensor_copy(srun_bf[:, H * 64:(H + HHALF) * 64],
                                srun[:, H * 64:(H + HHALF) * 64])
            b.on("vector", ff)
        b.stage(f"yst{c}")
        b.dma("sync", y_out[c * P:(c + 1) * P, :], ysb[:])
    b.stage("sendst")
    b.dma("sync", send_out, srun[:])
    b.finalize()
    return nc


def host_l2_inmaps(inp, l1_results):
    drep = _f32(np.repeat(np.asarray(inp["L0_attn_D"]), HEADDIM).reshape(1, D_SSM))
    tri = np.where(np.arange(P)[:, None] <= np.arange(P)[None, :], 1.0, 0.0
                   ).astype(np.float32)
    ones32 = np.ones((32, P), dtype=np.float32)
    import ml_dtypes
    identb = np.eye(P, dtype=ml_dtypes.bfloat16)
    identf = np.eye(P, dtype=np.float32)
    maps = []
    for c in range(NCORES):
        r = l1_results[c]
        maps.append({"xconv": _f32(r["xconv"]), "delta": _f32(r["delta"]),
                     "dA": _f32(r["dA"]), "d_rep": drep, "trimask": tri,
                     "ones_col": ones32[0:1], "ones32": ones32,
                     "ident_bf": identb, "ident_f": identf})
    return maps


def run_launch2(inp, l1_results):
    nc = build_launch2()
    res = run_bass_kernel_spmd(nc, host_l2_inmaps(inp, l1_results),
                               core_ids=list(range(NCORES)))
    return res.results



def build_launch3():
    nc = bass.Bass("TRN2", target_bir_lowering=False, detect_race_conditions=False)

    yl_in = _din(nc, "ylocal", [T, D_SSM])
    z_in = _din(nc, "z_tm", [T, D_SSM])
    xconv_in = _din(nc, "xconv", [CONV_DIM, T])
    lam_in = _din(nc, "lam", [T, NHEADS])
    atot_in = _din(nc, "atot", [NCH, NHEADS])
    sin_in = _din(nc, "s_in", [P, D_SSM])
    gamma_in = _din(nc, "gamma", [1, D_SSM])
    wout_in = _din(nc, "w_out", [D_SSM, D])
    onesc = _din(nc, "ones_col", [1, P])
    identb_in = _din(nc, "ident_bf", [P, P], BF16)

    s0_out = _dout(nc, "s0", [T, D])

    b = Builder(nc)
    cfm = b.sb("cfm", [P, T], BF16)
    sin = b.sb("sin", [P, D_SSM], F32)
    stl = b.sb("stl", [P, D_SSM], BF16)
    ytile = b.sb("ytile", [P, D_SSM], F32)
    ztile = b.sb("ztile", [P, D_SSM], F32)
    lam_tm = b.sb("lam_tm", [P, NHEADS], F32)
    elam = b.sb("elam", [P, NHEADS], F32)
    arow = b.sb("arow", [1, NHEADS], F32)
    abc = b.sb("abc", [P, NHEADS], F32)
    gbc = b.sb("gbc", [P, D_SSM], F32)
    grow = b.sb("grow", [1, D_SSM], F32)
    tscr = b.sb("tscr", [P, D_SSM], F32)
    prodfm = [b.sb(f"pfm{k}", [P, T], BF16) for k in range(16)]
    prod_bf = b.sb("prod_bf", [P, D_SSM], BF16)
    wout = [b.sb(f"wo{k}", [P, D], BF16) for k in range(16)]
    ones_sb = b.sb("onesc", [1, P], F32)
    identb = b.sb("idb", [P, P], BF16)
    rms = {k: b.sb(f"rms_{k}", [P, 1], F32) for k in ("ss", "var", "std", "rstd")}
    otile = b.sb("otile", [P, D], F32)

    pI = b.ps("pI", [P, 2048], F32)
    pO = b.ps("pO", [P, 1024], F32)
    pS = b.ps("pS", [P, 512], F32)
    pSb = pS[:].bitcast(BF16)

    b.stage("load")
    b.dma("gpsimd", cfm[:], xconv_in[D_SSM + P:D_SSM + 2 * P, :])
    b.dma("sync", sin[:], sin_in)
    b.dma("sync", grow[:], gamma_in)
    b.dma("sync", ones_sb[:], onesc)
    b.dma("sync", identb[:], identb_in)
    for k in range(16):
        b.dma("gpsimd", wout[k][:], wout_in[k * P:(k + 1) * P, :])

    for g in range(4):
        b.stage(f"gbc{g}")
        emit_bcast_row(b, pS[:, :], ones_sb[:], grow[0:1, g * 512:(g + 1) * 512])
        b.stage(f"gbce{g}")
        b.on("vector", lambda eng, g=g: eng.tensor_copy(gbc[:, g * 512:(g + 1) * 512], pS[:]))

    for c in range(NCH):
        b.stage(f"ld{c}")
        b.dma("sync", ytile[:], yl_in[c * P:(c + 1) * P, :])
        b.dma("sync", ztile[:], z_in[c * P:(c + 1) * P, :])
        b.dma("sync", lam_tm[:], lam_in[c * P:(c + 1) * P, :])
        b.dma("sync", arow[:], atot_in[c:c + 1, :])
        b.stage(f"abcm{c}")
        emit_bcast_row(b, pS[:, 0:NHEADS], ones_sb[:], arow[:])
        b.stage(f"abce{c}")
        def ab(eng):
            eng.tensor_copy(abc[:, :], pS[:, 0:NHEADS])
            eng.drain()
            a3 = abc[:].unsqueeze(2).broadcast_to((P, NHEADS, HEADDIM))
            eng.tensor_tensor(out=stl[:].rearrange("p (h q) -> p h q", q=HEADDIM),
                              in0=sin[:].rearrange("p (h q) -> p h q", q=HEADDIM),
                              in1=a3, op=ALU.mult)
        b.on("vector", ab)
        b.on("scalar", lambda eng: eng.activation(out=elam[:], in_=lam_tm[:], func=AF.Exp))
        b.stage(f"imm{c}")
        def im(eng, c=c):
            for g in range(4):
                eng.matmul(pI[:, g * 512:(g + 1) * 512], cfm[:, c * P:(c + 1) * P],
                           stl[:, g * 512:(g + 1) * 512], start=True, stop=True)
        b.on("tensor", im)
        b.on("scalar", lambda eng: eng.activation(out=ztile[:], in_=ztile[:], func=AF.Silu))
        b.stage(f"inj{c}")
        def ij(eng):
            el3 = elam[:].unsqueeze(2).broadcast_to((P, NHEADS, HEADDIM))
            eng.tensor_tensor(out=tscr[:].rearrange("p (h q) -> p h q", q=HEADDIM),
                              in0=pI[:].rearrange("p (h q) -> p h q", q=HEADDIM),
                              in1=el3, op=ALU.mult)
            eng.drain()
            eng.tensor_add(ytile[:], ytile[:], tscr[:])
        b.on("vector", ij)
        b.stage(f"sq{c}")
        b.on("scalar", lambda eng: eng.activation(out=tscr[:], in_=ytile[:],
                                                  func=AF.Square, accum_out=rms["ss"][:]))
        b.stage(f"rst{c}")
        def rs(eng):
            eng.tensor_scalar(out=rms["var"][:], in0=rms["ss"][:], scalar1=1.0 / D_SSM,
                              scalar2=1e-5, op0=ALU.mult, op1=ALU.add)
        b.on("vector", rs)
        b.stage(f"rst2{c}")
        b.on("scalar", lambda eng: eng.activation(out=rms["std"][:], in_=rms["var"][:],
                                                  func=AF.Sqrt))
        b.stage(f"rst3{c}")
        def rs3(eng):
            eng.reciprocal(rms["rstd"][:], rms["std"][:])
            eng.drain()
            eng.tensor_scalar_mul(ytile[:], ytile[:], rms["rstd"][:])
            eng.drain()
            eng.tensor_mul(ytile[:], ytile[:], gbc[:])
            eng.drain()
            eng.tensor_mul(prod_bf[:], ytile[:], ztile[:])
        b.on("vector", rs3)
        for half in range(2):
            b.stage(f"tr{c}_{half}")
            for j in range(8):
                kc = half * 8 + j
                def tr(eng, kc=kc, j=j):
                    eng.transpose(pI[:].bitcast(BF16)[:, j * 128:(j + 1) * 128],
                                  prod_bf[:, kc * 128:(kc + 1) * 128], identb[:])
                b.on("tensor", tr)
            b.stage(f"tre{c}_{half}")
            for j in range(8):
                kc = half * 8 + j
                def ev(eng, c=c, kc=kc, j=j):
                    eng.tensor_copy(prodfm[kc][:, c * P:(c + 1) * P],
                                    pI[:].bitcast(BF16)[:, j * 128:(j + 1) * 128])
                b.on("vector", ev)

    for i in range(NTT):
        for ng in range(3):
            b.stage(f"op{i}_{ng}")
            if ng < 2:
                for kc in range(16):
                    def mm(eng, i=i, ng=ng, kc=kc):
                        eng.matmul(pO[:, ng * 512:(ng + 1) * 512],
                                   prodfm[kc][:, i * P:(i + 1) * P],
                                   wout[kc][:, ng * 512:(ng + 1) * 512],
                                   start=(kc == 0), stop=(kc == 15))
                    b.on("tensor", mm)
            if ng > 0:
                def ev(eng, ng=ng):
                    eng.tensor_copy(otile[:, (ng - 1) * 512:ng * 512],
                                    pO[:, (ng - 1) * 512:ng * 512])
                b.on("vector", ev)
        b.stage(f"ops{i}")
        b.dma("sync", s0_out[i * P:(i + 1) * P, :], otile[:])
    b.finalize()
    return nc


def host_l3_inmaps(inp, l1, l2):
    import ml_dtypes
    gamma = _f32(np.asarray(inp["L0_attn_gamma"]).reshape(1, D_SSM))
    wout = _f32(inp["L0_attn_W_out"])
    ones = np.ones((1, P), dtype=np.float32)
    identb = np.eye(P, dtype=ml_dtypes.bfloat16)
    maps = []
    for c in range(NCORES):
        hh = c % 2
        sin = _f32(l2[c - 1]["send"]) if hh == 1 else np.zeros((P, D_SSM), np.float32)
        maps.append({"ylocal": _f32(l2[c]["ylocal"]), "z_tm": _f32(l1[c]["z_tm"]),
                     "xconv": _f32(l1[c]["xconv"]), "lam": _f32(l2[c]["lam"]),
                     "atot": _f32(l2[c]["atot"]), "s_in": sin, "gamma": gamma,
                     "w_out": wout, "ones_col": ones, "ident_bf": identb})
    return maps


def run_launch3(inp, l1, l2):
    nc = build_launch3()
    res = run_bass_kernel_spmd(nc, host_l3_inmaps(inp, l1, l2),
                               core_ids=list(range(NCORES)))
    return res.results


F0_HEADS = 8
F0_DH = 64
F0_INNER = 512
F0_WIN = 512
F0_HID = 2730
F0_KV = 3072
NEG = -1e30


def build_ffn0_layer():
    nc = bass.Bass("TRN2", target_bir_lowering=False, detect_race_conditions=False)

    xkv_in = _din(nc, "x_kv", [F0_KV, D])
    xprev_in = _din(nc, "xprev", [1, D])
    wqkv_in = _din(nc, "wqkv", [D, 3 * F0_INNER])
    wo_in = _din(nc, "wo", [F0_INNER, D])
    w1_in = _din(nc, "w1", [D, 2 * F0_HID])
    b1_in = _din(nc, "b1", [1, 2 * F0_HID])
    w2_in = _din(nc, "w2", [F0_HID, D])
    b2_in = _din(nc, "b2", [1, D])
    lng_in = _din(nc, "ln_g", [1, D])
    lnb_in = _din(nc, "ln_b", [1, D])
    qs_in = _din(nc, "qs_rep", [1, F0_INNER])
    ks_in = _din(nc, "ks_rep", [1, F0_INNER])
    tqc_in = _din(nc, "tqc", [F0_KV, F0_DH])
    tqs_in = _din(nc, "tqs", [F0_KV, F0_DH])
    tkc_in = _din(nc, "tkc", [F0_KV, F0_DH])
    tks_in = _din(nc, "tks", [F0_KV, F0_DH])
    msame_in = _din(nc, "mask_same", [4 * P, F0_WIN], BF16)
    mprevn_in = _din(nc, "mask_prevn", [4 * P, F0_WIN], BF16)
    mprev1_in = _din(nc, "mask_prev1", [4 * P, F0_WIN], BF16)
    mprev2_in = _din(nc, "mask_prev2", [4 * P, F0_WIN], BF16)
    onesc = _din(nc, "ones_col", [1, P])
    identb_in = _din(nc, "ident_bf", [P, P], BF16)

    xout_out = _dout(nc, "xout", [T, D])

    xm_dram = nc.dram_tensor("xm_dram", [F0_KV - F0_WIN, D], F32, kind="Internal")
    h1_dram = nc.dram_tensor("h1_dram", [T, 2 * F0_HID], F32, kind="Internal")

    b = Builder(nc)
    arena = b.sb("arena", [P, 41092], BF16)
    arF = arena[:].bitcast(F32)

    def abf(o, n):
        return arena[:, o:o + n]

    def af32(o, n):
        return arF[:, o // 2:o // 2 + n]

    XFM0 = 0
    QFM0 = 4120
    KFM0 = 6168
    VAUG0 = 10264
    OFM0 = 14424
    MSK0 = 16472
    TBL0 = 24664
    QKW0 = 28760
    SSC0 = 32856
    PTL0 = 36952
    OSB0 = 39000
    LINV0 = 40024
    assert LINV0 + 1024 <= 41092
    xfm = [abf(XFM0 + c * 514, 513) for c in range(DCH)]
    qfm = [abf(QFM0 + c * 512, 512) for c in range(4)]
    kfm = [abf(KFM0 + c * 1024, 1024) for c in range(4)]
    vaug = [abf(VAUG0 + i * 520, 520) for i in range(8)]
    ofm = [abf(OFM0 + c * 512, 512) for c in range(4)]
    masks = {"same": abf(MSK0, 2048), "prevn": abf(MSK0 + 2048, 2048),
             "prev1": abf(MSK0 + 4096, 2048), "prev2": abf(MSK0 + 6144, 2048)}
    tbl = {"tqc": af32(TBL0, 256), "tqs": af32(TBL0 + 1024, 256),
           "tkc": af32(TBL0 + 2048, 256), "tks": af32(TBL0 + 3072, 256)}
    qtile = af32(QKW0, 512)
    ktile = af32(QKW0 + 1024, 512)
    t1 = af32(QKW0 + 2048, 512)
    t2 = af32(QKW0 + 3072, 512)
    ssc = af32(SSC0, 2048)
    ptl = abf(PTL0, 2048)
    osb = af32(OSB0, 512)
    linv = af32(LINV0, 512)[0:1, :]
    hlnfm = [abf(c * 2048, 2048) for c in range(DCH)]
    hsh = af32(16384, 1024)
    hln = af32(18432, 1024)
    hlnbf = abf(20480, 1024)
    ablk = af32(21504, 512)
    gblk = af32(22528, 512)
    h2bf = abf(23552, 2730)
    h2fm = abf(26282 + 118, 2816)
    evt = af32(29216, 512)
    assert 29216 + 1024 <= 41092

    wqkv = [b.sb(f"wqkv{k}", [P, 3 * F0_INNER], BF16) for k in range(DCH)]
    wo = [b.sb(f"wo{k}", [P, D], BF16) for k in range(4)]
    w1s = [b.sb(f"w1s{k}", [P, 512], BF16) for k in range(DCH)]
    arena2 = b.sb("arena2", [P, 22528], BF16)
    w2 = [arena2[:, k * 1024:(k + 1) * 1024] for k in range(22)]
    xkb = b.sb("xkb", [P, D], BF16)
    xkv_t = b.sb("xkv_t", [P, D], F32)
    xres = b.sb("xres", [P, D], F32)
    xmid = b.sb("xmid", [P, D], F32)
    b1s = b.sb("b1s", [P, 512], F32)
    b2bc = b.sb("b2bc", [P, D], F32)
    qsbc = b.sb("qsbc", [P, F0_INNER], F32)
    ksbc = b.sb("ksbc", [P, F0_INNER], F32)
    lngbc = b.sb("lngbc", [P, D], F32)
    lnbbc = b.sb("lnbbc", [P, D], F32)
    nrm = b.sb("nrm", [P, F0_HEADS], F32)
    nrm2 = b.sb("nrm2", [P, F0_HEADS], F32)
    negm = b.sb("negm", [P, 1], F32)
    sc1 = b.sb("sc1", [1, F0_INNER], F32)
    sc2 = b.sb("sc2", [1, 4], F32)
    ones_sb = b.sb("onesc", [1, P], F32)
    identb = b.sb("idb", [P, P], BF16)
    ln = {k: b.sb(f"f0ln_{k}", [P, 1], F32)
          for k in ("m", "ss", "var", "std", "rstd", "m2")}
    ln["sq"] = b.sb("f0ln_sq", [P, D], F32)

    pq = b.ps("pq", [P, 1024], F32)
    pS = b.ps("pS", [P, 2048], F32)
    pO = b.ps("pO", [P, F0_WIN], F32)
    pX = b.ps("pX", [P, 512], F32)
    pXb = pX[:].bitcast(BF16)
    pSb = pS[:].bitcast(BF16)

    b.stage("load")
    for k in range(DCH):
        b.dma("gpsimd", wqkv[k][:], wqkv_in[k * P:(k + 1) * P, :])
    for k in range(4):
        b.dma("gpsimd", wo[k][:], wo_in[k * P:(k + 1) * P, :])
    b.dma("sync", masks["same"].rearrange("p (k w) -> p k w", k=4),
          msame_in.rearrange("(k p) w -> p k w", p=P))
    b.dma("sync", masks["prevn"].rearrange("p (k w) -> p k w", k=4),
          mprevn_in.rearrange("(k p) w -> p k w", p=P))
    b.dma("sync", masks["prev1"].rearrange("p (k w) -> p k w", k=4),
          mprev1_in.rearrange("(k p) w -> p k w", p=P))
    b.dma("sync", masks["prev2"].rearrange("p (k w) -> p k w", k=4),
          mprev2_in.rearrange("(k p) w -> p k w", p=P))
    b.dma("sync", ones_sb[:], onesc)
    b.dma("sync", identb[:], identb_in)
    for row_in, dst in ((qs_in, qsbc), (ks_in, ksbc)):
        b.stage("bc0")
        b.dma("sync", sc1[:], row_in)
        b.stage("bc")
        def mm(eng):
            eng.matmul(pX[:, 0:512], ones_sb[:], sc1[0:1, :], start=True, stop=True)
        b.on("tensor", mm)
        b.stage("bce")
        def ev(eng, dst=dst):
            eng.tensor_copy(dst[:, :], pX[:, 0:512])
        b.on("vector", ev)
    b.stage("mb1")
    b.dma("sync", sc1[:], qs_in)
    b.stage("mb2")
    b.on("vector", lambda eng: eng.reduce_max(sc2[:, 0:1], sc1[:], axis=AX.X,
                                              apply_absolute_value=True))
    b.stage("mb3")
    b.dma("sync", sc1[:], ks_in)
    b.stage("mb4")
    def mb4(eng):
        eng.reduce_max(sc2[:, 1:2], sc1[:], axis=AX.X, apply_absolute_value=True)
        eng.drain()
        eng.tensor_mul(sc2[:, 2:3], sc2[:, 0:1], sc2[:, 1:2])
        eng.drain()
        eng.tensor_scalar_mul(sc2[:, 2:3], sc2[:, 2:3], -8.0)
    b.on("vector", mb4)
    b.stage("mb5")
    b.on("tensor", lambda eng: eng.matmul(pX[:, 0:1], ones_sb[:], sc2[0:1, 2:3],
                                          start=True, stop=True))
    b.stage("mb6")
    b.on("vector", lambda eng: eng.tensor_copy(negm[:], pX[:, 0:1]))
    b.stage("xprev")
    b.dma("sync", xkv_t[0:1, :], xprev_in)
    b.stage("xprev1")
    b.on("vector", lambda eng: eng.tensor_copy(xkb[0:1, :], xkv_t[0:1, :]))
    b.stage("xprev2")
    def xp2(eng):
        for dc in range(DCH):
            eng.transpose(pXb[:, 2 * dc:2 * dc + 1], xkb[0:1, dc * P:(dc + 1) * P],
                          identb[0:1, 0:1])
    b.on("tensor", xp2)
    b.stage("xprev3")
    def xp3(eng):
        for dc in range(DCH):
            eng.tensor_copy(xfm[dc][:, 0:1], pXb[:, 2 * dc:2 * dc + 1])
    b.on("vector", xp3)

    def qk_process(which, i, w):
        src = qtile if which == "q" else ktile
        sbc = qsbc if which == "q" else ksbc
        tc = tbl["tqc" if which == "q" else "tkc"]
        ts_ = tbl["tqs" if which == "q" else "tks"]
        b.stage(f"{which}n{w}_{i}")
        def n1(eng):
            eng.tensor_mul(t1[:], src[:], src[:])
            eng.drain()
            eng.reduce_sum(nrm[:], t1[:].rearrange("p (h d) -> p h d", d=F0_DH),
                           axis=AX.X)
        b.on("vector", n1)
        b.stage(f"{which}n2{w}_{i}")
        b.on("scalar", lambda eng: eng.activation(out=nrm2[:], in_=nrm[:], func=AF.Sqrt))
        b.stage(f"{which}n3{w}_{i}")
        def n3(eng):
            eng.tensor_scalar_max(nrm2[:], nrm2[:], 1e-12)
            eng.drain()
            eng.reciprocal(nrm[:], nrm2[:])
            eng.drain()
            n3d = nrm[:].unsqueeze(2).broadcast_to((P, F0_HEADS, F0_DH))
            eng.tensor_tensor(out=t1[:].rearrange("p (h d) -> p h d", d=F0_DH),
                              in0=src[:].rearrange("p (h d) -> p h d", d=F0_DH),
                              in1=n3d, op=ALU.mult)
            eng.drain()
            eng.tensor_mul(t1[:], t1[:], sbc[:])
        b.on("vector", n3)
        b.stage(f"{which}r{w}_{i}")
        def rr(eng, i=i):
            tc3 = tc[:, i * F0_DH:(i + 1) * F0_DH].unsqueeze(1).broadcast_to(
                (P, F0_HEADS, F0_DH))
            ts3a = ts_[:, i * F0_DH:i * F0_DH + 32].unsqueeze(1).broadcast_to(
                (P, F0_HEADS, 32))
            ts3b = ts_[:, i * F0_DH + 32:(i + 1) * F0_DH].unsqueeze(1).broadcast_to(
                (P, F0_HEADS, 32))
            t13 = t1[:].rearrange("p (h d) -> p h d", d=F0_DH)
            t23 = t2[:].rearrange("p (h d) -> p h d", d=F0_DH)
            eng.tensor_tensor(out=t23[:, :, 0:32], in0=t13[:, :, 32:64], in1=ts3a,
                              op=ALU.mult)
            eng.tensor_tensor(out=t23[:, :, 32:64], in0=t13[:, :, 0:32], in1=ts3b,
                              op=ALU.mult)
            eng.drain()
            eng.tensor_tensor(out=t13, in0=t13, in1=tc3, op=ALU.mult)
            eng.drain()
            eng.tensor_add(t1[:], t1[:], t2[:])
        b.on("vector", rr)
        b.stage(f"{which}c{w}_{i}")
        b.on("vector", lambda eng: eng.tensor_copy(xkb[:, 0:512], t1[:]))
        b.stage(f"{which}t{w}_{i}")
        def tt_(eng):
            for qc in range(4):
                eng.transpose(pXb[:, qc * 128:(qc + 1) * 128],
                              xkb[:, 0:512].bitcast(BF16)[:, qc * 128:(qc + 1) * 128]
                              if False else xkb[:, qc * 128:(qc + 1) * 128],
                              identb[:])
        b.on("tensor", tt_)
        b.stage(f"{which}e{w}_{i}")
        def te(eng, which=which, i=i, w=w):
            for qc in range(4):
                if which == "q":
                    dst = qfm[qc][:, i * 128:(i + 1) * 128]
                else:
                    dst = kfm[qc][:, (w % 2) * 512 + i * 128:(w % 2) * 512 + (i + 1) * 128]
                eng.tensor_copy(dst, pXb[:, qc * 128:(qc + 1) * 128])
        b.on("vector", te)

    for w in range(6):
        wpar = w % 2
        b.stage(f"wld{w}")
        for tdst, tsrc in ((tbl["tqc"], tqc_in), (tbl["tqs"], tqs_in),
                           (tbl["tkc"], tkc_in), (tbl["tks"], tks_in)):
            b.dma("sync", tdst.rearrange("p (i d) -> p i d", i=4),
                  tsrc[w * F0_WIN:(w + 1) * F0_WIN, :].rearrange("(i p) d -> p i d", p=P))
        if w > 0:
            def car(eng):
                for dc in range(DCH):
                    eng.tensor_copy(xfm[dc][:, 0:1], xfm[dc][:, F0_WIN:F0_WIN + 1])
            b.on("vector", car)
        for i in range(4):
            b.stage(f"wx{w}_{i}")
            b.dma("sync", xkv_t[:], xkv_in[(4 * w + i) * P:(4 * w + i + 1) * P, :])
            b.stage(f"wc{w}_{i}")
            b.on("vector", lambda eng: eng.tensor_copy(xkb[:], xkv_t[:]))
            b.stage(f"wt{w}_{i}")
            def tr(eng):
                for dc in range(DCH):
                    pp = pXb if dc < 4 else pSb
                    eng.transpose(pp[:, (dc % 4) * 128:(dc % 4 + 1) * 128],
                                  xkb[:, dc * P:(dc + 1) * P], identb[:])
            b.on("tensor", tr)
            b.stage(f"we{w}_{i}")
            def ev(eng, i=i):
                for dc in range(DCH):
                    pp = pXb if dc < 4 else pSb
                    eng.tensor_copy(xfm[dc][:, 1 + i * 128:1 + (i + 1) * 128],
                                    pp[:, (dc % 4) * 128:(dc % 4 + 1) * 128])
            b.on("vector", ev)
            b.stage(f"qkv{w}_{i}")
            def qmm(eng, i=i):
                for dc in range(DCH):
                    c0 = i * 128 + (1 if dc < 4 else 0)
                    for g in range(3):
                        dst = pq[:, g * 512:(g + 1) * 512] if g < 2 else pS[:, 0:512]
                        eng.matmul(dst, xfm[dc][:, c0:c0 + 128],
                                   wqkv[dc][:, g * 512:(g + 1) * 512],
                                   start=(dc == 0), stop=(dc == DCH - 1))
            b.on("tensor", qmm)
            b.stage(f"qkve{w}_{i}")
            def qev(eng, i=i, wpar=wpar):
                va = vaug[wpar * 4 + i]
                eng.tensor_copy(va.rearrange("p (h d) -> p h d", d=65)[:, :, 0:64],
                                pS[:, 0:512].rearrange("p (h d) -> p h d", d=64))
                eng.memset(va.rearrange("p (h d) -> p h d", d=65)[:, :, 64:65], 1.0)
                eng.tensor_copy(qtile[:], pq[:, 0:512])
                eng.tensor_copy(ktile[:], pq[:, 512:1024])
            b.on("vector", qev)
            qk_process("q", i, w)
            qk_process("k", i, w)
        if w == 0:
            continue
        for h in range(F0_HEADS):
            hc, hp = h // 2, 64 * (h % 2)
            for ktg in range(2):
                b.stage(f"at{w}_{h}_{ktg}")
                def sa(eng, h=h, hc=hc, hp=hp, ktg=ktg, w=w, wpar=wpar):
                    for j in range(4):
                        kt = ktg * 4 + j
                        if kt < 4:
                            kcol = (1 - wpar) * 512 + kt * 128
                        else:
                            kcol = wpar * 512 + (kt - 4) * 128
                        eng.matmul(pS[:, j * 512:(j + 1) * 512],
                                   kfm[hc][hp:hp + 64, kcol:kcol + 128],
                                   qfm[hc][hp:hp + 64, :],
                                   start=True, stop=True)
                b.on("tensor", sa)
                b.stage(f"am{w}_{h}_{ktg}")
                def ma(eng, ktg=ktg, w=w):
                    for j in range(4):
                        kt = ktg * 4 + j
                        if kt < 4:
                            mk = (masks["prev1"] if w == 1 else
                                  masks["prev2"] if w == 2 else masks["prevn"])
                            msl = mk[:, kt * 512:(kt + 1) * 512]
                        else:
                            msl = masks["same"][:, (kt - 4) * 512:(kt - 3) * 512]
                        eng.tensor_add(ssc[:, j * 512:(j + 1) * 512],
                                       pS[:, j * 512:(j + 1) * 512], msl)
                b.on("vector", ma)
                b.stage(f"ax{w}_{h}_{ktg}")
                def ex(eng):
                    eng.activation(out=ptl[:], in_=ssc[:], func=AF.Exp, bias=negm[:],
                                   scale=8.0)
                b.on("scalar", ex)
                b.stage(f"av{w}_{h}_{ktg}")
                def pv(eng, h=h, ktg=ktg, wpar=wpar):
                    for j in range(4):
                        kt = ktg * 4 + j
                        vi = ((1 - wpar) * 4 + kt) if kt < 4 else (wpar * 4 + kt - 4)
                        eng.matmul(pO[0:65, :], vaug[vi][:, h * 65:(h + 1) * 65],
                                   ptl[:, j * 512:(j + 1) * 512],
                                   start=(ktg == 0 and j == 0),
                                   stop=(ktg == 1 and j == 3))
                b.on("tensor", pv)
            b.stage(f"ao{w}_{h}")
            def oe(eng):
                eng.tensor_copy(osb[0:64, :], pO[0:64, :])
                eng.reciprocal(linv, pO[64:65, :])
            b.on("vector", oe)
            b.stage(f"ab{w}_{h}")
            b.on("tensor", lambda eng: eng.matmul(pX[0:64, :], ones_sb[0:1, 0:64],
                                                  linv, start=True, stop=True))
            b.stage(f"af{w}_{h}")
            def of(eng, hc=hc, hp=hp):
                eng.tensor_mul(ofm[hc][hp:hp + 64, :], osb[0:64, :], pX[0:64, :])
            b.on("vector", of)
        for i in range(4):
            b.stage(f"wo{w}_{i}")
            b.dma("sync", xres[:], xkv_in[(4 * w + i) * P:(4 * w + i + 1) * P, :])
            def wmm(eng, i=i):
                for qc in range(4):
                    for g in range(2):
                        eng.matmul(pS[:, g * 512:(g + 1) * 512],
                                   ofm[qc][:, i * 128:(i + 1) * 128],
                                   wo[qc][:, g * 512:(g + 1) * 512],
                                   start=(qc == 0), stop=(qc == 3))
            b.on("tensor", wmm)
            b.stage(f"wor{w}_{i}")
            def wre(eng):
                eng.tensor_add(xmid[:, 0:512], pS[:, 0:512], xres[:, 0:512])
                eng.tensor_add(xmid[:, 512:1024], pS[:, 512:1024], xres[:, 512:1024])
            b.on("vector", wre)
            b.stage(f"wos{w}_{i}")
            b.dma("sync", xm_dram.ap()[((w - 1) * 4 + i) * P:((w - 1) * 4 + i + 1) * P, :],
                  xmid[:])

    for row_in, dst in ((lng_in, lngbc), (lnb_in, lnbbc), (b2_in, b2bc)):
        for g in range(2):
            b.stage("fbc0")
            b.dma("sync", sc1[:], row_in[0:1, g * 512:(g + 1) * 512])
            b.stage("fbc")
            def mm(eng):
                eng.matmul(pX[:, 0:512], ones_sb[:], sc1[0:1, :], start=True, stop=True)
            b.on("tensor", mm)
            b.stage("fbce")
            def ev(eng, dst=dst, g=g):
                eng.tensor_copy(dst[:, g * 512:(g + 1) * 512], pX[:, 0:512])
            b.on("vector", ev)
    for i in range(NTT):
        q0 = F0_WIN + i * P
        b.stage(f"f0l{i}")
        b.dma("sync", hsh[:, 0:512], xm_dram.ap()[q0:q0 + P, 0:512])
        b.dma("sync", hsh[:, 512:1024], xm_dram.ap()[q0 - 1:q0 + P - 1, 512:1024])
        emit_ln(b, hsh[:], hln[:], ln, D, gb=(lngbc[:], lnbbc[:]))
        b.stage(f"f0c{i}")
        b.on("vector", lambda eng: eng.tensor_copy(hlnbf[:, :], hln[:]))
        b.stage(f"f0t{i}")
        def f0t(eng):
            for dc in range(DCH):
                pp = pXb if dc < 4 else pSb
                eng.transpose(pp[:, (dc % 4) * 128:(dc % 4 + 1) * 128],
                              hlnbf[:, dc * 128:(dc + 1) * 128], identb[:])
        b.on("tensor", f0t)
        b.stage(f"f0e{i}")
        def f0e(eng, i=i):
            for dc in range(DCH):
                pp = pXb if dc < 4 else pSb
                eng.tensor_copy(hlnfm[dc][:, i * 128:(i + 1) * 128],
                                pp[:, (dc % 4) * 128:(dc % 4 + 1) * 128])
        b.on("vector", f0e)
    G1 = (2 * F0_HID + 511) // 512
    for g in range(G1):
        gw = min(512, 2 * F0_HID - g * 512)
        b.stage(f"f1l{g}")
        for k in range(DCH):
            b.dma("gpsimd", w1s[k][:, 0:gw], w1_in[k * P:(k + 1) * P,
                                                   g * 512:g * 512 + gw])
        def bmm(eng, g=g, gw=gw):
            eng.matmul(pO[:, 0:gw], ones_sb[:], b1_in if False else None) if False else None
        b.stage(f"f1b{g}")
        def bmm2(eng, g=g, gw=gw):
            eng.matmul(pO[:, 0:gw], ones_sb[:], None) if False else None
        b.dma("sync", sc1[0:1, 0:gw], b1_in[0:1, g * 512:g * 512 + gw])
        b.stage(f"f1bb{g}")
        def bmm3(eng, g=g, gw=gw):
            eng.matmul(pO[:, 0:gw], ones_sb[:], sc1[0:1, 0:gw], start=True, stop=True)
        b.on("tensor", bmm3)
        b.stage(f"f1bc{g}")
        b.on("vector", lambda eng, gw=gw: eng.tensor_copy(b1s[:, 0:gw], pO[:, 0:gw]))
        for i in range(NTT + 1):
            b.stage(f"f1m{g}_{i}")
            if i < NTT:
                pp = pX if i % 2 == 0 else pS
                def f1m(eng, g=g, gw=gw, i=i, pp=pp):
                    for dc in range(DCH):
                        eng.matmul(pp[:, 0:gw], hlnfm[dc][:, i * 128:(i + 1) * 128],
                                   w1s[dc][:, 0:gw], start=(dc == 0), stop=(dc == DCH - 1))
                b.on("tensor", f1m)
            if i > 0:
                pp2 = pX if (i - 1) % 2 == 0 else pS
                def f1e(eng, gw=gw, pp2=pp2):
                    eng.tensor_add(evt[:, 0:gw], pp2[:, 0:gw], b1s[:, 0:gw])
                b.on("vector", f1e)
            if i > 1:
                pass
            if i > 0:
                b.stage(f"f1s{g}_{i}")
                b.dma("sync", h1_dram.ap()[(i - 1) * P:i * P, g * 512:g * 512 + gw],
                      evt[:, 0:gw])
    b.stage("w2ld")
    for k in range(22):
        r0w = k * P
        r1w = min(F0_HID, r0w + P)
        b.dma("gpsimd", w2[k][0:r1w - r0w, :], w2_in[r0w:r1w, :])
    for i in range(NTT):
        row0 = i * P
        for blk in range(6):
            c0 = blk * 512
            cw = min(512, F0_HID - c0)
            b.stage(f"f2l{i}_{blk}")
            b.dma("sync", ablk[:, 0:cw], h1_dram.ap()[row0:row0 + P, c0:c0 + cw])
            b.dma("sync", gblk[:, 0:cw],
                  h1_dram.ap()[row0:row0 + P, F0_HID + c0:F0_HID + c0 + cw])
            b.stage(f"f2g{i}_{blk}")
            b.on("scalar", lambda eng, cw=cw: eng.activation(out=gblk[:, 0:cw],
                                                             in_=gblk[:, 0:cw],
                                                             func=AF.Gelu))
            b.stage(f"f2m{i}_{blk}")
            def f2m(eng, c0=c0, cw=cw):
                eng.tensor_mul(h2bf[:, c0:c0 + cw], ablk[:, 0:cw], gblk[:, 0:cw])
            b.on("vector", f2m)
        b.stage(f"f2t{i}")
        def f2t(eng):
            for k in range(22):
                r0w = k * P
                r1w = min(F0_HID, r0w + P)
                eng.transpose(pSb[0:r1w - r0w, k * 128:(k + 1) * 128],
                              h2bf[:, r0w:r1w], identb[:])
        b.on("tensor", f2t)
        b.stage(f"f2te{i}")
        def f2te(eng):
            for k in range(22):
                r0w = k * P
                r1w = min(F0_HID, r0w + P)
                eng.tensor_copy(h2fm[0:r1w - r0w, k * 128:(k + 1) * 128],
                                pSb[0:r1w - r0w, k * 128:(k + 1) * 128])
        b.on("vector", f2te)
        b.stage(f"f2w{i}")
        b.dma("sync", xres[:], xm_dram.ap()[F0_WIN + row0:F0_WIN + row0 + P, :])
        def f2w(eng):
            for k in range(22):
                kk = min(P, F0_HID - k * P)
                for g in range(2):
                    eng.matmul(pq[:, g * 512:(g + 1) * 512],
                               h2fm[0:kk, k * 128:k * 128 + 128],
                               w2[k][0:kk, g * 512:(g + 1) * 512],
                               start=(k == 0), stop=(k == 21))
        b.on("tensor", f2w)
        b.stage(f"f2r{i}")
        def f2r(eng):
            eng.tensor_add(xmid[:, 0:512], pq[:, 0:512], b2bc[:, 0:512])
            eng.tensor_add(xmid[:, 512:1024], pq[:, 512:1024], b2bc[:, 512:1024])
            eng.drain()
            eng.tensor_add(xmid[:], xmid[:], xres[:])
        b.on("vector", f2r)
        b.stage(f"f2s{i}")
        b.dma("sync", xout_out[row0:row0 + P, :], xmid[:])
    b.finalize()
    return nc


def _rotary_tables(g):
    half = np.arange(0, 64, 2, dtype=np.float64)
    inv_freq = 1.0 / 10000.0 ** (half / 64.0)
    fr = g[:, None] * inv_freq[None, :]
    fr = np.concatenate([fr, fr], axis=1)
    base = (half + 0.4 * 64) / (1.4 * 64)
    base = np.concatenate([base, base])
    power = (g - 4096 // 2) / float(F0_WIN // 2)
    scale = base[None, :] ** power[:, None]
    cos, sin = np.cos(fr), np.sin(fr)
    sgn = np.concatenate([-np.ones(32), np.ones(32)])[None, :]
    shuf = np.concatenate([sin[:, 32:64] * 0 + sin[:, 0:32] * 0 + sin[:, 0:32],
                           sin[:, 32:64]], axis=1)
    tqc = cos * scale
    tqs = sgn * sin * scale
    tkc = cos / scale
    tks = sgn * sin / scale
    return (tqc.astype(np.float32), tqs.astype(np.float32),
            tkc.astype(np.float32), tks.astype(np.float32))


def host_ffn0_inmaps(inp, l, x_full):
    import ml_dtypes
    wqkv = _f32(inp["L0_ffn_wqkv"][l])
    wo = _f32(inp["L0_ffn_wo"][l])
    w1 = _f32(inp["L0_ffn_w1"][l])
    b1 = _f32(inp["L0_ffn_b1"][l].reshape(1, 2 * F0_HID))
    w2 = _f32(inp["L0_ffn_w2"][l])
    b2 = _f32(inp["L0_ffn_b2"][l].reshape(1, D))
    lng = _f32(inp["L0_ffn_ln_g"][l].reshape(1, D))
    lnb = _f32(inp["L0_ffn_ln_b"][l].reshape(1, D))
    qs = _f32(np.tile(inp["L0_ffn_q_scale"][l], F0_HEADS).reshape(1, F0_INNER))
    ks = _f32(np.tile(inp["L0_ffn_k_scale"][l], F0_HEADS).reshape(1, F0_INNER))
    jl = np.arange(F0_WIN)[:, None]
    il = np.arange(F0_WIN)[None, :]
    msame = np.where(il < jl, NEG, 0.0).astype(ml_dtypes.bfloat16)
    mprevn = np.where(il > jl, NEG, 0.0).astype(ml_dtypes.bfloat16)
    mfull = np.full((F0_WIN, F0_WIN), NEG, dtype=ml_dtypes.bfloat16)
    ones = np.ones((1, P), dtype=np.float32)
    identb = np.eye(P, dtype=ml_dtypes.bfloat16)
    maps = []
    for c in range(NCORES):
        bb, hh = c // 2, c % 2
        r0 = hh * T
        xkv = np.zeros((F0_KV, D), dtype=np.float32)
        lo = max(0, r0 - 1024)
        xkv[1024 - (r0 - lo):] = x_full[bb, lo:r0 + T]
        xprev = np.zeros((1, D), dtype=np.float32)
        if r0 - 1025 >= 0:
            xprev[0] = x_full[bb, r0 - 1025]
        g = np.arange(F0_KV, dtype=np.float64) + (r0 - 1024)
        tqc, tqs, tkc, tks = _rotary_tables(g)
        maps.append({
            "x_kv": xkv, "xprev": xprev, "wqkv": wqkv, "wo": wo, "w1": w1,
            "b1": b1, "w2": w2, "b2": b2, "ln_g": lng, "ln_b": lnb,
            "qs_rep": qs, "ks_rep": ks, "tqc": tqc, "tqs": tqs, "tkc": tkc,
            "tks": tks, "mask_same": msame, "mask_prevn": mprevn,
            "mask_prev1": (mprevn if hh == 1 else mfull),
            "mask_prev2": (mprevn if hh == 1 else mfull),
            "ones_col": ones, "ident_bf": identb})
    return maps


def run_ffn0(inp, l, x_full):
    nc = build_ffn0_layer()
    res = run_bass_kernel_spmd(nc, host_ffn0_inmaps(inp, l, x_full),
                               core_ids=list(range(NCORES)))
    return res.results


def assemble(results, key, dtype=np.float32):
    n = results[0][key].shape[-1]
    out = np.zeros((4, 4096, n), dtype=dtype)
    for c in range(NCORES):
        bb, hh = c // 2, c % 2
        out[bb, hh * T:(hh + 1) * T] = results[c][key]
    return out



def build_launch6():
    nc = bass.Bass("TRN2", target_bir_lowering=False, detect_race_conditions=False)
    x_in = _din(nc, "x", [T, D])
    s0_in = _din(nc, "s0", [T, D])
    g0_in = _din(nc, "g0", [T, D])
    x1_out = _dout(nc, "x1", [T, D])
    xn1_out = _dout(nc, "xn1", [T, D])
    b = Builder(nc)
    xt = b.sb("xt", [P, D], F32)
    st_ = b.sb("st_", [P, D], F32)
    gt = b.sb("gt", [P, D], F32)
    x1t = b.sb("x1t", [P, D], F32)
    xnt = b.sb("xnt", [P, D], F32)
    ln = {k: b.sb(f"l6_{k}", [P, 1], F32)
          for k in ("m", "ss", "var", "std", "rstd", "m2")}
    ln["sq"] = b.sb("l6_sq", [P, D], F32)
    for i in range(NTT):
        b.stage(f"ld{i}")
        b.dma("sync", xt[:], x_in[i * P:(i + 1) * P, :])
        b.dma("sync", st_[:], s0_in[i * P:(i + 1) * P, :])
        b.dma("sync", gt[:], g0_in[i * P:(i + 1) * P, :])
        b.stage(f"sl{i}")
        b.on("scalar", lambda eng: eng.activation(out=gt[:], in_=gt[:], func=AF.Silu))
        b.stage(f"cb{i}")
        def cb(eng):
            eng.tensor_mul(st_[:], st_[:], gt[:])
            eng.drain()
            eng.tensor_add(x1t[:], xt[:], st_[:])
        b.on("vector", cb)
        emit_ln(b, x1t[:], xnt[:], ln, D)
        b.stage(f"st{i}")
        b.dma("sync", x1_out[i * P:(i + 1) * P, :], x1t[:])
        b.dma("sync", xn1_out[i * P:(i + 1) * P, :], xnt[:])
    b.finalize()
    return nc


def run_launch6(x_own, s0res, g0res):
    nc = build_launch6()
    maps = [{"x": x_own[c], "s0": _f32(s0res[c]["s0"]), "g0": _f32(g0res[c]["xout"])}
            for c in range(NCORES)]
    res = run_bass_kernel_spmd(nc, maps, core_ids=list(range(NCORES)))
    return res.results


CH = 16
CHD = 64


def build_launch7a():
    nc = bass.Bass("TRN2", target_bir_lowering=False, detect_race_conditions=False)
    xn_in = _din(nc, "xn1", [T, D])
    cq_in = _din(nc, "cq", [D, D])
    ck_in = _din(nc, "ck", [D, D])
    cv_in = _din(nc, "cv", [D, D])
    bq_in = _din(nc, "bq", [1, D])
    bk_in = _din(nc, "bk", [1, D])
    bv_in = _din(nc, "bv", [1, D])
    onesc = _din(nc, "ones_col", [1, P])
    identb_in = _din(nc, "ident_bf", [P, P], BF16)
    identf_in = _din(nc, "ident_f", [P, P], F32)

    qfm_out = _dout(nc, "qfm", [D, T], BF16)
    kfm_out = _dout(nc, "kfm", [D, T], BF16)
    vtm_out = _dout(nc, "vtm", [T, D], BF16)
    qn_out = _dout(nc, "qn", [1, CH])
    kn_out = _dout(nc, "kn", [1, CH])

    b = Builder(nc)
    xnfm = [b.sb(f"xnfm{c}", [P, T], BF16) for c in range(DCH)]
    cw = [b.sb(f"cw{c}", [P, D], BF16) for c in range(DCH)]
    xnt = b.sb("xnt", [P, D], F32)
    xnb = b.sb("xnb", [P, D], BF16)
    qt = b.sb("qt", [P, D], F32)
    qb = b.sb("qb", [P, D], BF16)
    sq_ = b.sb("sq_", [P, D], F32)
    nr = b.sb("nr", [P, CH], F32)
    nmax = b.sb("nmax", [CH, 1], F32)
    nrow = b.sb("nrow", [1, CH], F32)
    bbc = b.sb("bbc", [P, D], F32)
    brow = b.sb("brow", [1, D], F32)
    ones_sb = b.sb("onesc", [1, P], F32)
    identb = b.sb("idb", [P, P], BF16)
    identf = b.sb("idf", [P, P], F32)
    pA = b.ps("pA", [P, 1024], F32)
    pB = b.ps("pB", [P, 1024], F32)
    pT = b.ps("pT", [P, 512], F32)
    pTb = pT[:].bitcast(BF16)

    b.stage("load")
    b.dma("sync", ones_sb[:], onesc)
    b.dma("sync", identb[:], identb_in)
    b.dma("sync", identf[:], identf_in)
    for i in range(NTT):
        b.stage(f"xl{i}")
        b.dma("sync", xnt[:], xn_in[i * P:(i + 1) * P, :])
        b.stage(f"xc{i}")
        b.on("vector", lambda eng: eng.tensor_copy(xnb[:], xnt[:]))
        b.stage(f"xt{i}")
        def tr(eng):
            for dc in range(DCH):
                pp = pTb if dc < 4 else pA[:].bitcast(BF16)
                eng.transpose(pp[:, (dc % 4) * 128:(dc % 4 + 1) * 128],
                              xnb[:, dc * 128:(dc + 1) * 128], identb[:])
        b.on("tensor", tr)
        b.stage(f"xe{i}")
        def ev(eng, i=i):
            for dc in range(DCH):
                pp = pTb if dc < 4 else pA[:].bitcast(BF16)
                eng.tensor_copy(xnfm[dc][:, i * P:(i + 1) * P],
                                pp[:, (dc % 4) * 128:(dc % 4 + 1) * 128])
        b.on("vector", ev)

    for proj, (c_in, b_in, fm_out, tm_out, n_out) in enumerate(
            [(cq_in, bq_in, qfm_out, None, qn_out),
             (ck_in, bk_in, kfm_out, None, kn_out),
             (cv_in, bv_in, None, vtm_out, None)]):
        b.stage(f"p{proj}ld")
        for c in range(DCH):
            b.dma("gpsimd", cw[c][:], c_in[c * P:(c + 1) * P, :])
        b.dma("sync", brow[:], b_in)
        b.stage(f"p{proj}bc1")
        b.on("tensor", lambda eng: eng.matmul(pB[:, 0:512], ones_sb[:], brow[0:1, 0:512],
                                              start=True, stop=True))
        b.stage(f"p{proj}bc2")
        b.on("vector", lambda eng: eng.tensor_copy(bbc[:, 0:512], pB[:, 0:512]))
        b.stage(f"p{proj}bc3")
        b.on("tensor", lambda eng: eng.matmul(pB[:, 0:512], ones_sb[:], brow[0:1, 512:1024],
                                              start=True, stop=True))
        b.stage(f"p{proj}bc4")
        b.on("vector", lambda eng: eng.tensor_copy(bbc[:, 512:1024], pB[:, 0:512]))
        if n_out is not None:
            b.stage(f"p{proj}nm")
            b.on("vector", lambda eng: eng.memset(nmax[:], 0.0))
        for i in range(NTT):
            b.stage(f"p{proj}m{i}")
            def mm(eng, i=i):
                for ng in range(2):
                    for kc in range(DCH):
                        eng.matmul(pA[:, ng * 512:(ng + 1) * 512],
                                   xnfm[kc][:, i * P:(i + 1) * P],
                                   cw[kc][:, ng * 512:(ng + 1) * 512],
                                   start=(kc == 0), stop=(kc == DCH - 1))
            b.on("tensor", mm)
            b.stage(f"p{proj}e{i}")
            def ev(eng):
                eng.tensor_add(qt[:], pA[:, 0:1024], bbc[:])
                eng.drain()
                eng.tensor_copy(qb[:], qt[:])
            b.on("vector", ev)
            if tm_out is not None:
                b.stage(f"p{proj}s{i}")
                b.dma("sync", tm_out[i * P:(i + 1) * P, :], qb[:])
                continue
            b.stage(f"p{proj}n{i}")
            def nrm_(eng):
                eng.tensor_mul(sq_[:], qt[:], qt[:])
                eng.drain()
                eng.reduce_sum(nr[:], sq_[:].rearrange("p (h d) -> p h d", d=CHD),
                               axis=AX.X)
            b.on("vector", nrm_)
            b.stage(f"p{proj}n2{i}")
            b.on("tensor", lambda eng: eng.transpose(pB[0:CH, 0:128], nr[:, :],
                                                     identf[:]))
            b.stage(f"p{proj}n3{i}")
            def nmx(eng):
                eng.reduce_max(nr[0:CH, 0:1], pB[0:CH, 0:128], axis=AX.X)
                eng.drain()
                eng.tensor_max(nmax[:], nmax[:], nr[0:CH, 0:1])
            b.on("vector", nmx)
            b.stage(f"p{proj}t{i}")
            def tr2(eng):
                for dc in range(DCH):
                    pp = pTb if dc < 4 else pB[:].bitcast(BF16)
                    eng.transpose(pp[:, (dc % 4) * 128:(dc % 4 + 1) * 128],
                                  qb[:, dc * 128:(dc + 1) * 128], identb[:])
            b.on("tensor", tr2)
            b.stage(f"p{proj}te{i}")
            def tev(eng, i=i):
                for dc in range(DCH):
                    pp = pTb if dc < 4 else pB[:].bitcast(BF16)
                    eng.tensor_copy(xnb[:, dc * 128:(dc + 1) * 128],
                                    pp[:, (dc % 4) * 128:(dc % 4 + 1) * 128])
            b.on("vector", tev)
            b.stage(f"p{proj}ts{i}")
            b.dma("sync", fm_out.rearrange("(c p) t -> p c t", p=P)[:, :, i * P:(i + 1) * P],
                  xnb[:].rearrange("p (c q) -> p c q", q=P))
        if n_out is not None:
            b.stage(f"p{proj}f1")
            b.on("scalar", lambda eng: eng.activation(out=nmax[:], in_=nmax[:],
                                                      func=AF.Sqrt))
            b.stage(f"p{proj}f2")
            b.on("tensor", lambda eng: eng.transpose(pB[0:1, 0:CH], nmax[:, :],
                                                     identf[0:CH, 0:CH]))
            b.stage(f"p{proj}f3")
            b.on("vector", lambda eng: eng.tensor_copy(nrow[:], pB[0:1, 0:CH]))
            b.stage(f"p{proj}f4")
            b.dma("sync", n_out, nrow[:])
    b.finalize()
    return nc


def host_l7a_inmaps(inp, l6res):
    import ml_dtypes
    def circ(c):
        cvec = np.asarray(c, dtype=np.float32)
        idx = (np.arange(D)[None, :] - np.arange(D)[:, None]) % D
        return np.ascontiguousarray(cvec[idx])
    CQ = circ(inp["L1_attn_wq_c"]); CK = circ(inp["L1_attn_wk_c"])
    CV = circ(inp["L1_attn_wv_c"])
    ones = np.ones((1, P), dtype=np.float32)
    identb = np.eye(P, dtype=ml_dtypes.bfloat16)
    maps = []
    for c in range(NCORES):
        maps.append({"xn1": _f32(l6res[c]["xn1"]), "cq": CQ, "ck": CK, "cv": CV,
                     "bq": _f32(inp["L1_attn_wq_b"].reshape(1, D)),
                     "bk": _f32(inp["L1_attn_wk_b"].reshape(1, D)),
                     "bv": _f32(inp["L1_attn_wv_b"].reshape(1, D)),
                     "ones_col": ones, "ident_bf": identb,
                     "ident_f": np.eye(P, dtype=np.float32)})
    return maps


def run_launch7a(inp, l6res):
    nc = build_launch7a()
    res = run_bass_kernel_spmd(nc, host_l7a_inmaps(inp, l6res),
                               core_ids=list(range(NCORES)))
    return res.results


TK = 4096


def build_launch7b():
    nc = bass.Bass("TRN2", target_bir_lowering=False, detect_race_conditions=False)
    qfm_in = _din(nc, "qfm", [D, T], BF16)
    kfm_in = _din(nc, "kfm_full", [D, TK], BF16)
    vaug_in = _din(nc, "vaug_full", [TK, CH * 65], BF16)
    qn_in = _din(nc, "qn", [1, CH])
    kn_in = _din(nc, "kn2", [1, 2 * CH])
    co_in = _din(nc, "co", [D, D])
    bo_in = _din(nc, "bo", [1, D])
    onesc = _din(nc, "ones_col", [1, P])
    s1_out = _dout(nc, "s1", [T, D])

    b = Builder(nc)
    arena7 = b.sb("arena7", [P, 16384], BF16)
    qfm = [arena7[:, c * 2048:(c + 1) * 2048] for c in range(DCH)]
    co = [arena7[:, c * 1024:(c + 1) * 1024] for c in range(DCH)]
    ar7f = arena7[:].bitcast(F32)
    bobc = ar7f[:, 4096:5120]
    s1t = ar7f[:, 5120:6144]
    brow = ar7f[0:1, 6144:7168]
    kfm = [b.sb(f"kfm{c}", [P, TK], BF16) for c in range(DCH)]
    vaug = [b.sb(f"va{i}", [P, CH * 65], BF16) for i in range(32)]
    ofm = [b.sb(f"ofm{c}", [P, T], BF16) for c in range(DCH)]
    ptl = [b.sb(f"ptl{i}", [P, 1024], BF16) for i in range(2)]
    osb = b.sb("osb", [CHD, 1024], F32)
    linv = b.sb("linv", [1, 1024], F32)
    negmbc = b.sb("negmbc", [P, CH], F32)
    nrow = b.sb("nrow", [1, CH], F32)
    nrow2 = b.sb("nrow2", [1, 2 * CH], F32)
    ones_sb = b.sb("onesc", [1, P], F32)
    pS = b.ps("pS", [P, 2048], F32)
    pO = b.ps("pO", [P, 1024], F32)
    pX = b.ps("pX", [P, 512], F32)

    b.stage("load")
    b.dma("sync", ones_sb[:], onesc)
    for c in range(DCH):
        b.dma("sync", qfm[c], qfm_in[c * P:(c + 1) * P, :])
        b.dma("sync", kfm[c][:], kfm_in[c * P:(c + 1) * P, :])
    for i in range(32):
        b.dma("sync", vaug[i][:], vaug_in[i * P:(i + 1) * P, :])
    b.dma("sync", nrow[:], qn_in)
    b.dma("sync", nrow2[:], kn_in)
    b.stage("m1")
    def m1(eng):
        eng.tensor_max(nrow2[0:1, 0:CH], nrow2[0:1, 0:CH], nrow2[0:1, CH:2 * CH])
        eng.drain()
        eng.tensor_mul(nrow[:], nrow[:], nrow2[0:1, 0:CH])
        eng.drain()
        eng.tensor_scalar_mul(nrow[:], nrow[:], -0.125)
    b.on("vector", m1)
    b.stage("m2")
    b.on("tensor", lambda eng: eng.matmul(pX[:, 0:CH], ones_sb[:], nrow[0:1, :],
                                          start=True, stop=True))
    b.stage("m3")
    b.on("vector", lambda eng: eng.tensor_copy(negmbc[:], pX[:, 0:CH]))

    for h in range(CH):
        hc, hp = h // 2, 64 * (h % 2)
        for qb in range(2):
            q0 = qb * 1024
            for kt in range(34):
                b.stage(f"a{h}_{qb}_{kt}")
                def tstep(eng, h=h, hc=hc, hp=hp, kt=kt, q0=q0):
                    if kt < 32:
                        for g in range(2):
                            eng.matmul(pS[:, (kt % 2) * 1024 + g * 512:
                                          (kt % 2) * 1024 + (g + 1) * 512],
                                       kfm[hc][hp:hp + 64, kt * P:(kt + 1) * P],
                                       qfm[hc][hp:hp + 64, q0 + g * 512:q0 + (g + 1) * 512],
                                       start=True, stop=True)
                    if kt >= 2:
                        kk = kt - 2
                        for g in range(2):
                            eng.matmul(pO[0:65, g * 512:(g + 1) * 512],
                                       vaug[kk][:, h * 65:(h + 1) * 65],
                                       ptl[kk % 2][:, g * 512:(g + 1) * 512],
                                       start=(kk == 0), stop=(kk == 31))
                b.on("tensor", tstep)
                def sstep(eng, h=h, kt=kt):
                    if 1 <= kt <= 32:
                        eng.activation(out=ptl[(kt - 1) % 2][:],
                                       in_=pS[:, ((kt - 1) % 2) * 1024:
                                              ((kt - 1) % 2 + 1) * 1024],
                                       func=AF.Exp, scale=0.125,
                                       bias=negmbc[:, h:h + 1])
                b.on("scalar", sstep)
            b.stage(f"d1{h}_{qb}")
            def d1(eng):
                eng.tensor_copy(osb[:, :], pO[0:CHD, :])
                eng.reciprocal(linv[:], pO[CHD:CHD + 1, 0:1024])
            b.on("vector", d1)
            for g in range(2):
                b.stage(f"d2{h}_{qb}_{g}")
                def d2(eng, g=g):
                    eng.matmul(pX[0:CHD, :], ones_sb[0:1, 0:CHD],
                               linv[0:1, g * 512:(g + 1) * 512], start=True, stop=True)
                b.on("tensor", d2)
                b.stage(f"d3{h}_{qb}_{g}")
                def d3(eng, hc=hc, hp=hp, q0=q0, g=g):
                    eng.tensor_mul(ofm[hc][hp:hp + 64, q0 + g * 512:q0 + (g + 1) * 512],
                                   osb[:, g * 512:(g + 1) * 512], pX[0:CHD, :])
                b.on("vector", d3)
    b.stage("cold")
    for c in range(DCH):
        b.dma("gpsimd", co[c], co_in[c * P:(c + 1) * P, :])
    b.dma("sync", brow, bo_in)
    for g in range(2):
        b.stage(f"cobc{g}")
        def mm(eng, g=g):
            eng.matmul(pX[:, :], ones_sb[:], brow[0:1, g * 512:(g + 1) * 512],
                       start=True, stop=True)
        b.on("tensor", mm)
        b.stage(f"cobce{g}")
        b.on("vector", lambda eng, g=g: eng.tensor_copy(bobc[:, g * 512:(g + 1) * 512],
                                                        pX[:, :]))
    for i in range(NTT):
        b.stage(f"o{i}")
        def omm(eng, i=i):
            for ng in range(2):
                for kc in range(DCH):
                    eng.matmul(pO[:, ng * 512:(ng + 1) * 512],
                               ofm[kc][:, i * P:(i + 1) * P],
                               co[kc][:, ng * 512:(ng + 1) * 512],
                               start=(kc == 0), stop=(kc == DCH - 1))
        b.on("tensor", omm)
        b.stage(f"oe{i}")
        def oev(eng):
            eng.tensor_add(s1t[:], pO[:, 0:1024], bobc[:])
        b.on("vector", oev)
        b.stage(f"os{i}")
        b.dma("sync", s1_out[i * P:(i + 1) * P, :], s1t[:])
    b.finalize()
    return nc


def host_l7b_inmaps(inp, l7a):
    import ml_dtypes
    def circ(c):
        cvec = np.asarray(c, dtype=np.float32)
        idx = (np.arange(D)[None, :] - np.arange(D)[:, None]) % D
        return np.ascontiguousarray(cvec[idx])
    CO = circ(inp["L1_attn_wo_c"])
    ones = np.ones((1, P), dtype=np.float32)
    maps = []
    for c in range(NCORES):
        bb, hh = c // 2, c % 2
        peer = 2 * bb + (1 - hh)
        kfull = np.concatenate([l7a[2 * bb][ "kfm"], l7a[2 * bb + 1]["kfm"]], axis=1)
        vfull = np.concatenate([l7a[2 * bb]["vtm"], l7a[2 * bb + 1]["vtm"]], axis=0)
        va = np.ones((TK, CH, 65), dtype=ml_dtypes.bfloat16)
        va[:, :, 0:64] = vfull.reshape(TK, CH, 64)
        kn2 = np.concatenate([l7a[2 * bb]["kn"], l7a[2 * bb + 1]["kn"]], axis=1)
        maps.append({"qfm": l7a[c]["qfm"], "kfm_full": kfull,
                     "vaug_full": np.ascontiguousarray(va.reshape(TK, CH * 65)),
                     "qn": _f32(l7a[c]["qn"]), "kn2": _f32(kn2),
                     "co": CO, "bo": _f32(inp["L1_attn_wo_b"].reshape(1, D)),
                     "ones_col": ones})
    return maps


def run_launch7b(inp, l7a):
    nc = build_launch7b()
    res = run_bass_kernel_spmd(nc, host_l7b_inmaps(inp, l7a),
                               core_ids=list(range(NCORES)))
    return res.results


NE = 8
HH = 128


def emit_gating(b, logits_psum_getter, simfm, rinv_bc, sgbc, n1rn, rw_tile, gt):
    pass


def build_launch8a():
    nc = bass.Bass("TRN2", target_bir_lowering=False, detect_race_conditions=False)
    xn_in = _din(nc, "xn1", [T, D])
    ln1w_in = _din(nc, "ln1_w", [1, D])
    ln1b_in = _din(nc, "ln1_b", [1, D])
    sim_in = _din(nc, "smha_sim", [D, NE])
    gates_in = _din(nc, "smha_gates", [1, NE])
    qall_in = _din(nc, "q_all", [D, NE * HH])
    kall_in = _din(nc, "k_all", [D, NE * HH])
    vall_in = _din(nc, "v_all", [D, NE * HH])
    onesc = _din(nc, "ones_col", [1, P])
    identb_in = _din(nc, "ident_bf", [P, P], BF16)

    rw_out = _dout(nc, "rw", [T, NE])
    qm_out = _dout(nc, "qmix", [T, HH])
    km_out = _dout(nc, "kmix", [T, HH])
    vm_out = _dout(nc, "vmix", [T, HH])
    n1_out = _dout(nc, "n1", [T, D])
    qn_out = _dout(nc, "qn8", [1, 1])
    kn_out = _dout(nc, "kn8", [1, 1])

    b = Builder(nc)
    n1fm = [b.sb(f"n1fm{c}", [P, T], BF16) for c in range(DCH)]
    wq = [b.sb(f"wq{c}", [P, NE * HH], BF16) for c in range(DCH)]
    wk = [b.sb(f"wk{c}", [P, NE * HH], BF16) for c in range(DCH)]
    wv = [b.sb(f"wv{c}", [P, NE * HH], BF16) for c in range(DCH)]
    simf = [b.sb(f"simf{c}", [P, NE], BF16) for c in range(DCH)]
    simf32 = [b.sb(f"simf32_{c}", [P, NE], F32) for c in range(DCH)]
    n1f32 = b.sb("n1f32", [P, D], F32)
    simsq = b.sb("simsq", [P, NE], F32)
    xnt = b.sb("xnt", [P, D], F32)
    n1t = b.sb("n1t", [P, D], F32)
    n1b = b.sb("n1b", [P, D], BF16)
    lgw = b.sb("lgw", [P, NE], F32)
    gtd = b.sb("gtd", [P, NE], F32)
    msk = b.sb("msk", [P, NE], F32)
    wexp = b.sb("wexp", [P, NE], F32)
    st1 = b.sb("st1", [P, 1], F32)
    st2 = b.sb("st2", [P, 1], F32)
    st3 = b.sb("st3", [P, 1], F32)
    rn = b.sb("rn", [P, 1], F32)
    lngbc = b.sb("lngbc", [P, D], F32)
    lnbbc = b.sb("lnbbc", [P, D], F32)
    sgbc = b.sb("sgbc", [P, NE], F32)
    rivbc = b.sb("rivbc", [P, NE], F32)
    srow = b.sb("srow", [1, NE], F32)
    srow2 = b.sb("srow2", [1, NE], F32)
    brow = b.sb("brow", [1, D], F32)
    mx = b.sb("mx", [P, HH], F32)
    nmxq = b.sb("nmxq", [P, 1], F32)
    nmxk = b.sb("nmxk", [P, 1], F32)
    gmx = b.sb("gmx", [1, 1], F32)
    ones_sb = b.sb("onesc", [1, P], F32)
    ones_cp = b.sb("onescp", [P, 1], F32)
    identb = b.sb("idb", [P, P], BF16)
    identf = b.sb("idf", [P, P], F32)
    ln = {k: b.sb(f"l8_{k}", [P, 1], F32)
          for k in ("m", "ss", "var", "std", "rstd", "m2")}
    ln["sq"] = b.sb("l8_sq", [P, D], F32)
    pA = b.ps("pA", [P, 1024], F32)
    pB = b.ps("pB", [P, 1024], F32)
    pT = b.ps("pT", [P, 512], F32)
    pTb = pT[:].bitcast(BF16)

    b.stage("load")
    b.dma("sync", ones_sb[:], onesc)
    b.dma("sync", identb[:], identb_in)
    for c in range(DCH):
        b.dma("gpsimd", wq[c][:], qall_in[c * P:(c + 1) * P, :])
        b.dma("gpsimd", wk[c][:], kall_in[c * P:(c + 1) * P, :])
        b.dma("gpsimd", wv[c][:], vall_in[c * P:(c + 1) * P, :])
        b.dma("gpsimd", simf[c][:], sim_in[c * P:(c + 1) * P, :])
        b.dma("sync", simf32[c][:], sim_in[c * P:(c + 1) * P, :])
    b.dma("sync", brow[:], ln1w_in)
    b.stage("idf")
    def _idf(eng):
        eng.tensor_copy(identf[:], identb[:])
        eng.memset(ones_cp[:], 1.0)
    b.on("vector", _idf)
    for g in range(2):
        b.stage(f"g{g}")
        b.on("tensor", lambda eng, g=g: eng.matmul(pT[:], ones_sb[:],
                                                   brow[0:1, g * 512:(g + 1) * 512],
                                                   start=True, stop=True))
        b.stage(f"ge{g}")
        b.on("vector", lambda eng, g=g: eng.tensor_copy(lngbc[:, g * 512:(g + 1) * 512],
                                                        pT[:]))
    b.stage("lb0")
    b.dma("sync", brow[:], ln1b_in)
    for g in range(2):
        b.stage(f"g2{g}")
        b.on("tensor", lambda eng, g=g: eng.matmul(pT[:], ones_sb[:],
                                                   brow[0:1, g * 512:(g + 1) * 512],
                                                   start=True, stop=True))
        b.stage(f"g2e{g}")
        b.on("vector", lambda eng, g=g: eng.tensor_copy(lnbbc[:, g * 512:(g + 1) * 512],
                                                        pT[:]))
    b.stage("sn1")
    def sn1(eng):
        for c in range(DCH):
            eng.tensor_mul(simsq[:, :], simf[c][:], simf[c][:])
            eng.drain()
    b.on("vector", None) if False else None
    for c in range(DCH):
        b.stage(f"sq{c}")
        def sq(eng, c=c):
            eng.tensor_mul(simsq[:, :], simf[c][:], simf[c][:])
        b.on("vector", sq)
        b.stage(f"sm{c}")
        def sm(eng, c=c):
            eng.matmul(pT[0:1, 0:NE], ones_cp[:, :], simsq[:, :],
                       start=(c == 0), stop=(c == DCH - 1))
        b.on("tensor", sm)
    b.stage("sn2")
    def sn2(eng):
        eng.tensor_copy(srow[:], pT[0:1, 0:NE])
    b.on("vector", sn2)
    b.stage("sn3")
    b.on("scalar", lambda eng: eng.activation(out=srow[:], in_=srow[:], func=AF.Sqrt))
    b.stage("sn4")
    def sn4(eng):
        eng.tensor_scalar_max(srow[:], srow[:], 1e-12)
        eng.drain()
        eng.reciprocal(srow[:], srow[:])
    b.on("vector", sn4)
    b.stage("sn5")
    b.on("tensor", lambda eng: eng.matmul(pT[:, 0:NE], ones_sb[:], srow[0:1, :],
                                          start=True, stop=True))
    b.stage("sn6")
    b.on("vector", lambda eng: eng.tensor_copy(rivbc[:], pT[:, 0:NE]))
    b.stage("sg1")
    b.dma("sync", srow2[:], gates_in)
    b.stage("sg2")
    b.on("scalar", lambda eng: eng.activation(out=srow2[:], in_=srow2[:], func=AF.Sigmoid))
    b.stage("sg3")
    b.on("tensor", lambda eng: eng.matmul(pT[:, 0:NE], ones_sb[:], srow2[0:1, :],
                                          start=True, stop=True))
    b.stage("sg4")
    b.on("vector", lambda eng: eng.tensor_copy(sgbc[:], pT[:, 0:NE]))
    b.stage("nm0")
    def nm0(eng):
        eng.memset(nmxq[:], 0.0)
        eng.memset(nmxk[:], 0.0)
    b.on("vector", nm0)

    for i in range(NTT):
        b.stage(f"ld{i}")
        b.dma("sync", xnt[:], xn_in[i * P:(i + 1) * P, :])
        emit_ln(b, xnt[:], n1t[:], ln, D, gb=(lngbc[:], lnbbc[:]))
        b.stage(f"cs{i}")
        def cs(eng):
            eng.tensor_copy(n1b[:], n1t[:])
        b.on("vector", cs)
        b.dma("sync", n1_out[i * P:(i + 1) * P, :], n1t[:])
        b.stage(f"tr{i}")
        def tr(eng):
            for dc in range(DCH):
                pp = pTb if dc < 4 else pA[:].bitcast(BF16)
                eng.transpose(pp[:, (dc % 4) * 128:(dc % 4 + 1) * 128],
                              n1b[:, dc * 128:(dc + 1) * 128], identb[:])
        b.on("tensor", tr)
        b.stage(f"te{i}")
        def te(eng, i=i):
            for dc in range(DCH):
                pp = pTb if dc < 4 else pA[:].bitcast(BF16)
                eng.tensor_copy(n1fm[dc][:, i * P:(i + 1) * P],
                                pp[:, (dc % 4) * 128:(dc % 4 + 1) * 128])
        b.on("vector", te)
        for hb in range(2):
            b.stage(f"tf{i}_{hb}")
            def tf(eng, hb=hb):
                for q in range(4):
                    eng.transpose(pB[:, q * 128:(q + 1) * 128],
                                  n1t[:, (hb * 4 + q) * 128:(hb * 4 + q + 1) * 128],
                                  identf[:])
            b.on("tensor", tf)
            b.stage(f"tfe{i}_{hb}")
            def tfe(eng, hb=hb):
                for q in range(4):
                    eng.tensor_copy(n1f32[:, (hb * 4 + q) * 128:(hb * 4 + q + 1) * 128],
                                    pB[:, q * 128:(q + 1) * 128])
            b.on("vector", tfe)
        b.stage(f"lg{i}")
        def lg(eng):
            for kc in range(DCH):
                eng.matmul(pT[:, 0:NE], n1f32[:, kc * 128:(kc + 1) * 128], simf32[kc][:],
                           start=(kc == 0), stop=(kc == DCH - 1))
        b.on("tensor", lg)
        def rnorm(eng):
            eng.tensor_mul(ln["sq"][:], n1t[:], n1t[:])
            eng.drain()
            eng.reduce_sum(st1[:], ln["sq"][:], axis=AX.X)
        b.on("vector", rnorm)
        b.stage(f"rn2{i}")
        b.on("scalar", lambda eng: eng.activation(out=st2[:], in_=st1[:], func=AF.Sqrt))
        b.stage(f"rn3{i}")
        def rn3(eng):
            eng.tensor_scalar_max(st2[:], st2[:], 1e-12)
            eng.drain()
            eng.reciprocal(rn[:], st2[:])
            eng.drain()
            eng.tensor_mul(lgw[:], pT[:, 0:NE], rivbc[:])
            eng.drain()
            eng.tensor_scalar_mul(lgw[:], lgw[:], rn[:])
            eng.drain()
            eng.tensor_sub(lgw[:], lgw[:], sgbc[:])
        b.on("vector", rn3)
        b.stage(f"gw{i}")
        def gw(eng):
            eng.tensor_scalar_max(gtd[:], lgw[:], 0.0)
            eng.reduce_max(st1[:], lgw[:], axis=AX.X)
            eng.drain()
            eng.tensor_scalar(out=msk[:], in0=lgw[:], scalar1=st1[:], scalar2=None,
                              op0=ALU.is_ge, op1=ALU.bypass)
            eng.drain()
            eng.scalar_tensor_tensor(out=msk[:], in0=msk[:], scalar=NEG, in1=lgw[:],
                                     op0=ALU.mult, op1=ALU.add)
            eng.drain()
            eng.reduce_max(st2[:], msk[:], axis=AX.X)
            eng.drain()
            eng.tensor_scalar(out=msk[:], in0=lgw[:], scalar1=st2[:], scalar2=None,
                              op0=ALU.is_ge, op1=ALU.bypass)
            eng.tensor_scalar(out=wexp[:], in0=lgw[:], scalar1=0.0, scalar2=None,
                              op0=ALU.is_gt, op1=ALU.bypass)
            eng.tensor_scalar(out=st3[:], in0=st1[:], scalar1=0.0, scalar2=None,
                              op0=ALU.is_gt, op1=ALU.bypass)
            eng.drain()
            eng.tensor_sub(wexp[:], wexp[:], msk[:])
            eng.drain()
            eng.tensor_scalar_mul(wexp[:], wexp[:], st3[:])
            eng.drain()
            eng.tensor_add(msk[:], msk[:], wexp[:])
            eng.tensor_scalar_max(st1[:], st1[:], 0.0)
            eng.drain()
            eng.tensor_scalar_mul(st1[:], st1[:], -1.0)
        b.on("vector", gw)
        b.stage(f"gw2{i}")
        b.on("scalar", lambda eng: eng.activation(out=wexp[:], in_=gtd[:], func=AF.Exp,
                                                  bias=st1[:]))
        b.stage(f"gw3{i}")
        def gw3(eng):
            eng.tensor_mul(wexp[:], wexp[:], msk[:])
            eng.drain()
            eng.reduce_sum(st2[:], wexp[:], axis=AX.X)
            eng.drain()
            eng.reciprocal(st3[:], st2[:])
            eng.drain()
            eng.tensor_scalar_mul(wexp[:], wexp[:], st3[:])
        b.on("vector", gw3)
        b.stage(f"gs{i}")
        b.dma("sync", rw_out[i * P:(i + 1) * P, :], wexp[:])
        for which, wgt, out_t in (("q", wq, qm_out), ("k", wk, km_out), ("v", wv, vm_out)):
            b.stage(f"{which}m{i}")
            def qm(eng, i=i, wgt=wgt):
                for ng in range(2):
                    for kc in range(DCH):
                        eng.matmul(pA[:, ng * 512:(ng + 1) * 512],
                                   n1fm[kc][:, i * P:(i + 1) * P],
                                   wgt[kc][:, ng * 512:(ng + 1) * 512],
                                   start=(kc == 0), stop=(kc == DCH - 1))
            b.on("tensor", qm)
            b.stage(f"{which}x{i}")
            def qx(eng):
                rw3 = wexp[:].unsqueeze(2).broadcast_to((P, NE, HH))
                eng.tensor_tensor(out=ln["sq"][:].rearrange("p (e h) -> p e h", h=HH),
                                  in0=pA[:].rearrange("p (e h) -> p e h", h=HH),
                                  in1=rw3, op=ALU.mult)
                eng.drain()
                src3 = bass.AP(ln["sq"][:].tensor, ln["sq"][:].offset,
                               [[1, P], [1, HH], [HH, NE]]) if False else None
                eng.tensor_reduce(mx[:, :],
                                  ln["sq"][:].rearrange("p (e h) -> p h e", h=HH),
                                  axis=AX.X, op=ALU.add)
            b.on("vector", qx)
            b.stage(f"{which}s{i}")
            b.dma("sync", out_t[i * P:(i + 1) * P, :], mx[:])
            if which in ("q", "k"):
                nm = nmxq if which == "q" else nmxk
                b.stage(f"{which}n{i}")
                def qn(eng, nm=nm):
                    eng.tensor_mul(ln["sq"][:, 0:HH], mx[:], mx[:])
                    eng.drain()
                    eng.reduce_sum(st1[:], ln["sq"][:, 0:HH], axis=AX.X)
                    eng.drain()
                    eng.tensor_max(nm[:], nm[:], st1[:])
                b.on("vector", qn)
    for nm, out_t in ((nmxq, qn_out), (nmxk, kn_out)):
        b.stage("fin1")
        b.on("tensor", lambda eng, nm=nm: eng.transpose(pT[0:1, 0:P], nm[:, :],
                                                        identf[:]))
        b.stage("fin2")
        def f2(eng):
            eng.reduce_max(st1[0:1, 0:1], pT[0:1, 0:P], axis=AX.X)
        b.on("vector", f2)
        b.stage("fin3")
        b.on("scalar", lambda eng: eng.activation(out=gmx[:], in_=st1[0:1, 0:1],
                                                  func=AF.Sqrt))
        b.stage("fin4")
        b.dma("sync", out_t, gmx[:])
    b.finalize()
    return nc


def host_l8a_inmaps(inp, l6res):
    import ml_dtypes
    qall = _f32(np.asarray(inp["L1_ffn_q_proj"]).transpose(1, 0, 2).reshape(D, NE * HH))
    kall = _f32(np.asarray(inp["L1_ffn_k_proj"]).transpose(1, 0, 2).reshape(D, NE * HH))
    vall = _f32(np.asarray(inp["L1_ffn_v_proj"]).transpose(1, 0, 2).reshape(D, NE * HH))
    ones = np.ones((1, P), dtype=np.float32)
    identb = np.eye(P, dtype=ml_dtypes.bfloat16)
    maps = []
    for c in range(NCORES):
        maps.append({"xn1": _f32(l6res[c]["xn1"]),
                     "ln1_w": _f32(inp["L1_ffn_ln1_w"].reshape(1, D)),
                     "ln1_b": _f32(inp["L1_ffn_ln1_b"].reshape(1, D)),
                     "smha_sim": _f32(inp["L1_ffn_smha_sim"]),
                     "smha_gates": _f32(inp["L1_ffn_smha_gates"].reshape(1, NE)),
                     "q_all": qall, "k_all": kall, "v_all": vall,
                     "ones_col": ones, "ident_bf": identb})
    return maps


def run_launch8a(inp, l6res):
    nc = build_launch8a()
    res = run_bass_kernel_spmd(nc, host_l8a_inmaps(inp, l6res),
                               core_ids=list(range(NCORES)))
    return res.results



def build_launch8b():
    nc = bass.Bass("TRN2", target_bir_lowering=False, detect_race_conditions=False)
    qm_in = _din(nc, "qmix", [T, HH])
    kf_in = _din(nc, "kfull", [2 * T, HH])
    vf_in = _din(nc, "vfull", [2 * T, HH + 1], BF16)
    rw_in = _din(nc, "rw", [T, NE])
    oall_in = _din(nc, "o_all", [HH, NE * D])
    x1_in = _din(nc, "x1", [T, D])
    ln2w_in = _din(nc, "ln2_w", [1, D])
    ln2b_in = _din(nc, "ln2_b", [1, D])
    msim_in = _din(nc, "moe_sim", [D, NE])
    mgates_in = _din(nc, "moe_gates", [1, NE])
    qn_in = _din(nc, "qn8", [1, 1])
    kn_in = _din(nc, "kn8", [1, 2])
    pbias_in = _din(nc, "peer_bias", [1, 1])
    mdiag_in = _din(nc, "mask_diag", [8 * P, 1024], BF16)
    onesc = _din(nc, "ones_col", [1, P])
    identb_in = _din(nc, "ident_bf", [P, P], BF16)

    hs_out = _dout(nc, "hs", [T, D])
    n2_out = _dout(nc, "n2", [T, D])
    rw2_out = _dout(nc, "rw2", [T, NE])

    b = Builder(nc)
    kfm = b.sb("kfm", [P, 2 * T], BF16)
    qfm = b.sb("qfm", [P, T], BF16)
    vtm = [b.sb(f"vtm{i}", [P, HH + 1], BF16) for i in range(32)]
    ofmT = b.sb("ofmT", [P, T], BF16)
    mdiag = b.sb("mdiag", [P, 8 * 1024], BF16)
    oall = b.sb("oall", [P, NE * D], BF16)
    rwl = b.sb("rwl", [P, NE], F32)
    rwt = b.sb("rwt", [P, NE], F32)
    qt = b.sb("qt", [P, HH], F32)
    qb_ = b.sb("qb_", [P, HH], BF16)
    ptl = [b.sb(f"ptl{i}", [P, 1024], BF16) for i in range(1)]
    lrow = b.sb("lrow", [1, 1024], F32)
    ltm = b.sb("ltm", [P, 8], F32)
    acc = [b.sb(f"acc{i}", [P, D], F32) for i in range(NTT)]
    x1t = b.sb("x1t", [P, D], F32)
    hst = b.sb("hst", [P, D], F32)
    n2t = b.sb("n2t", [P, D], F32)
    simf = [b.sb(f"simf{c}", [P, NE], BF16) for c in range(DCH)]
    simf32 = [b.sb(f"simf32_{c}", [P, NE], F32) for c in range(DCH)]
    n2f32 = b.sb("n2f32", [P, D], F32)
    simsq = b.sb("simsq", [P, NE], F32)
    n2b = b.sb("n2b", [P, D], BF16)
    lgw = b.sb("lgw", [P, NE], F32)
    gtd = b.sb("gtd", [P, NE], F32)
    msk = b.sb("msk", [P, NE], F32)
    wexp = b.sb("wexp", [P, NE], F32)
    st1 = b.sb("st1", [P, 1], F32)
    st2 = b.sb("st2", [P, 1], F32)
    st3 = b.sb("st3", [P, 1], F32)
    rn = b.sb("rn", [P, 1], F32)
    lngbc = b.sb("lngbc", [P, D], F32)
    lnbbc = b.sb("lnbbc", [P, D], F32)
    sgbc = b.sb("sgbc", [P, NE], F32)
    rivbc = b.sb("rivbc", [P, NE], F32)
    srow = b.sb("srow", [1, NE], F32)
    srow2 = b.sb("srow2", [1, NE], F32)
    brow = b.sb("brow", [1, D], F32)
    nq = b.sb("nq", [1, 2], F32)
    negb = b.sb("negb", [P, 2], F32)
    ones_sb = b.sb("onesc", [1, P], F32)
    ones_cp = b.sb("onescp", [P, 1], F32)
    identb = b.sb("idb", [P, P], BF16)
    identf = b.sb("idf", [P, P], F32)
    ln = {k: b.sb(f"l8b_{k}", [P, 1], F32)
          for k in ("m", "ss", "var", "std", "rstd", "m2")}
    ln["sq"] = b.sb("l8b_sq", [P, D], F32)
    pS = b.ps("pS", [P, 1024], F32)
    pO = b.ps("pO", [P, 1024], F32)
    pL = b.ps("pL", [P, 1024], F32)
    pT = b.ps("pT", [P, 512], F32)
    pTb = pT[:].bitcast(BF16)

    b.stage("load")
    b.dma("sync", ones_sb[:], onesc)
    b.dma("sync", identb[:], identb_in)
    for i in range(32):
        b.dma("sync", vtm[i][:], vf_in[i * P:(i + 1) * P, :])
    b.dma("sync", mdiag[:].rearrange("p (k w) -> p k w", k=8),
          mdiag_in.rearrange("(k p) w -> p k w", p=P))
    for c in range(DCH):
        b.dma("gpsimd", simf[c][:], msim_in[c * P:(c + 1) * P, :])
        b.dma("sync", simf32[c][:], msim_in[c * P:(c + 1) * P, :])
    b.dma("gpsimd", oall[:], oall_in)
    b.dma("sync", nq[0:1, 0:1], qn_in)
    b.stage("idf")
    def _idf(eng):
        eng.tensor_copy(identf[:], identb[:])
        eng.memset(ones_cp[:], 1.0)
    b.on("vector", _idf)
    for i in range(NTT):
        b.stage(f"ql{i}")
        b.dma("sync", qt[:], qm_in[i * P:(i + 1) * P, :])
        b.stage(f"qc{i}")
        b.on("vector", lambda eng: eng.tensor_copy(qb_[:], qt[:]))
        b.stage(f"qt{i}")
        b.on("tensor", lambda eng: eng.transpose(pTb[:, 0:128], qb_[:, :], identb[:]))
        b.stage(f"qe{i}")
        b.on("vector", lambda eng, i=i: eng.tensor_copy(qfm[:, i * P:(i + 1) * P],
                                                        pTb[:, 0:128]))
    for i in range(32):
        b.stage(f"kl{i}")
        b.dma("sync", qt[:], kf_in[i * P:(i + 1) * P, :])
        b.stage(f"kc{i}")
        b.on("vector", lambda eng: eng.tensor_copy(qb_[:], qt[:]))
        b.stage(f"kt{i}")
        b.on("tensor", lambda eng: eng.transpose(pTb[:, 0:128], qb_[:, :], identb[:]))
        b.stage(f"ke{i}")
        b.on("vector", lambda eng, i=i: eng.tensor_copy(kfm[:, i * P:(i + 1) * P],
                                                        pTb[:, 0:128]))
    b.stage("mb0")
    b.dma("sync", nq[0:1, 1:2] if False else srow[0:1, 0:2], kn_in)
    b.stage("mb1")
    def mb1(eng):
        eng.tensor_max(srow[0:1, 0:1], srow[0:1, 0:1], srow[0:1, 1:2])
        eng.drain()
        eng.tensor_mul(nq[0:1, 0:1], nq[0:1, 0:1], srow[0:1, 0:1])
        eng.drain()
        eng.tensor_scalar_mul(nq[0:1, 0:1], nq[0:1, 0:1], -1.0 / float(np.sqrt(HH)))
        eng.drain()
        eng.tensor_copy(nq[0:1, 1:2], nq[0:1, 0:1])
    b.on("vector", mb1)
    b.stage("mb2")
    b.dma("sync", srow[0:1, 0:1], pbias_in)
    b.stage("mb3")
    def mb3(eng):
        eng.tensor_add(nq[0:1, 1:2], nq[0:1, 1:2], srow[0:1, 0:1])
    b.on("vector", mb3)
    b.stage("mb4")
    b.on("tensor", lambda eng: eng.matmul(pT[:, 0:2], ones_sb[:], nq[0:1, 0:2],
                                          start=True, stop=True))
    b.stage("mb5")
    b.on("vector", lambda eng: eng.tensor_copy(negb[:], pT[:, 0:2]))

    scale = 1.0 / float(np.sqrt(HH))
    linvall0 = b.sb("linvall0", [P, 8], F32)
    linvall1 = b.sb("linvall1", [P, 8], F32)
    for qb in range(2):
        q0 = qb * 1024
        kts = list(range(0, 8 * (qb + 1))) + list(range(16, 32))
        nk = len(kts)
        for s in range(nk + 1):
            b.stage(f"b{qb}_{s}")
            def tstep(eng, s=s, kts=kts, nk=nk, q0=q0):
                if s < nk:
                    kt = kts[s]
                    for g in range(2):
                        eng.matmul(pS[:, g * 512:(g + 1) * 512],
                                   kfm[:, kt * P:(kt + 1) * P],
                                   qfm[:, q0 + g * 512:q0 + (g + 1) * 512],
                                   start=True, stop=True)
                if s >= 1:
                    ss = s - 1
                    kt = kts[ss]
                    for g in range(2):
                        eng.matmul(pO[0:HH, g * 512:(g + 1) * 512],
                                   vtm[kt][:, 0:HH], ptl[0][:, g * 512:(g + 1) * 512],
                                   start=(ss == 0), stop=(ss == nk - 1))
                        eng.matmul(pL[0:1, g * 512:(g + 1) * 512],
                                   vtm[kt][:, HH:HH + 1], ptl[0][:, g * 512:(g + 1) * 512],
                                   start=(ss == 0), stop=(ss == nk - 1))
            b.on("tensor", tstep)
            if s < nk:
                b.stage(f"bm{qb}_{s}")
                def vstep(eng, s=s, kts=kts, qb=qb):
                    kt = kts[s]
                    own_diag = kt < 16 and (kt >= 8 * qb)
                    if own_diag:
                        dix = kt - 8 * qb
                        eng.tensor_add(ln["sq"][:, 0:1024], pS[:, 0:1024],
                                       mdiag[:, dix * 1024:(dix + 1) * 1024])
                    else:
                        eng.tensor_copy(ln["sq"][:, 0:1024], pS[:, 0:1024])
                b.on("vector", vstep)
                b.stage(f"bx{qb}_{s}")
                def sstep(eng, s=s, kts=kts):
                    kt = kts[s]
                    bias = negb[:, 1:2] if kt >= 16 else negb[:, 0:1]
                    eng.activation(out=ptl[0][:], in_=ln["sq"][:, 0:1024],
                                   func=AF.Exp, scale=scale, bias=bias)
                b.on("scalar", sstep)
        b.stage(f"dv{qb}")
        def dv(eng):
            eng.reciprocal(lrow[:], pL[0:1, :])
        b.on("vector", dv)
        b.stage(f"dt{qb}")
        def dtq(eng):
            for a in range(8):
                eng.transpose(pT[:, a:a + 1], lrow[0:1, a * P:(a + 1) * P],
                              identf[0:1, 0:1])
        b.on("tensor", dtq)
        b.stage(f"de{qb}")
        lv = linvall0 if qb == 0 else linvall1
        def de(eng, lv=lv):
            eng.tensor_copy(lv[:, 0:8], pT[0:P, 0:8])
        b.on("vector", de)
        b.stage(f"oc{qb}")
        def oc(eng, q0=q0):
            eng.tensor_copy(ofmT[:, q0:q0 + 1024], pO[0:HH, :])
        b.on("vector", oc)
    b.stage("lnb")
    b.dma("sync", brow[:], ln2w_in)
    for g in range(2):
        b.stage(f"g{g}")
        b.on("tensor", lambda eng, g=g: eng.matmul(pT[:], ones_sb[:],
                                                   brow[0:1, g * 512:(g + 1) * 512],
                                                   start=True, stop=True))
        b.stage(f"ge{g}")
        b.on("vector", lambda eng, g=g: eng.tensor_copy(lngbc[:, g * 512:(g + 1) * 512],
                                                        pT[:]))
    b.stage("lnb2")
    b.dma("sync", brow[:], ln2b_in)
    for g in range(2):
        b.stage(f"h{g}")
        b.on("tensor", lambda eng, g=g: eng.matmul(pT[:], ones_sb[:],
                                                   brow[0:1, g * 512:(g + 1) * 512],
                                                   start=True, stop=True))
        b.stage(f"he{g}")
        b.on("vector", lambda eng, g=g: eng.tensor_copy(lnbbc[:, g * 512:(g + 1) * 512],
                                                        pT[:]))
    for c in range(DCH):
        b.stage(f"sq{c}")
        b.on("vector", lambda eng, c=c: eng.tensor_mul(simsq[:, :], simf[c][:], simf[c][:]))
        b.stage(f"sm{c}")
        b.on("tensor", lambda eng, c=c: eng.matmul(pT[0:1, 0:NE], ones_cp[:, :],
                                                   simsq[:, :], start=(c == 0),
                                                   stop=(c == DCH - 1)))
    b.stage("sn2")
    b.on("vector", lambda eng: eng.tensor_copy(srow[:], pT[0:1, 0:NE]))
    b.stage("sn3")
    b.on("scalar", lambda eng: eng.activation(out=srow[:], in_=srow[:], func=AF.Sqrt))
    b.stage("sn4")
    def sn4(eng):
        eng.tensor_scalar_max(srow[:], srow[:], 1e-12)
        eng.drain()
        eng.reciprocal(srow[:], srow[:])
    b.on("vector", sn4)
    b.stage("sn5")
    b.on("tensor", lambda eng: eng.matmul(pT[:, 0:NE], ones_sb[:], srow[0:1, :],
                                          start=True, stop=True))
    b.stage("sn6")
    b.on("vector", lambda eng: eng.tensor_copy(rivbc[:], pT[:, 0:NE]))
    b.stage("sg1")
    b.dma("sync", srow2[:], mgates_in)
    b.stage("sg2")
    b.on("scalar", lambda eng: eng.activation(out=srow2[:], in_=srow2[:], func=AF.Sigmoid))
    b.stage("sg3")
    b.on("tensor", lambda eng: eng.matmul(pT[:, 0:NE], ones_sb[:], srow2[0:1, :],
                                          start=True, stop=True))
    b.stage("sg4")
    b.on("vector", lambda eng: eng.tensor_copy(sgbc[:], pT[:, 0:NE]))

    for i in range(NTT):
        qb = i // 8
        lcol = (linvall0 if qb == 0 else linvall1)[:, i % 8:i % 8 + 1]
        b.stage(f"r{i}")
        b.dma("sync", rwt[:], rw_in[i * P:(i + 1) * P, :])
        b.dma("sync", x1t[:], x1_in[i * P:(i + 1) * P, :])
        b.stage(f"rl{i}")
        def rl(eng, lcol=lcol):
            eng.tensor_scalar_mul(rwl[:], rwt[:], lcol)
        b.on("vector", rl)
        for eg in range(NE):
            b.stage(f"om{i}_{eg}")
            def om(eng, i=i, eg=eg):
                for g in range(2):
                    eng.matmul(pS[:, g * 512:(g + 1) * 512],
                               ofmT[:, i * P:(i + 1) * P],
                               oall[:, eg * 1024 + g * 512:eg * 1024 + (g + 1) * 512],
                               start=True, stop=True)
            b.on("tensor", om)
            b.stage(f"ox{i}_{eg}")
            def ox(eng, i=i, eg=eg):
                e = eg
                if eg == 0:
                    eng.tensor_scalar(out=acc[i][:], in0=pS[:, 0:1024],
                                      scalar1=rwl[:, e:e + 1], scalar2=None,
                                      op0=ALU.mult, op1=ALU.bypass)
                else:
                    eng.scalar_tensor_tensor(out=acc[i][:], in0=pS[:, 0:1024],
                                             scalar=rwl[:, e:e + 1], in1=acc[i][:],
                                             op0=ALU.mult, op1=ALU.add)
            b.on("vector", ox)
        b.stage(f"hs{i}")
        def hsf(eng, i=i):
            eng.tensor_add(hst[:], x1t[:], acc[i][:])
        b.on("vector", hsf)
        emit_ln(b, hst[:], n2t[:], ln, D, gb=(lngbc[:], lnbbc[:]))
        b.stage(f"hx{i}")
        b.dma("sync", hs_out[i * P:(i + 1) * P, :], hst[:])
        b.dma("sync", n2_out[i * P:(i + 1) * P, :], n2t[:])
        for hb in range(2):
            b.stage(f"tf{i}_{hb}")
            def tf(eng, hb=hb):
                for q in range(4):
                    eng.transpose(pS[:, q * 128:(q + 1) * 128],
                                  n2t[:, (hb * 4 + q) * 128:(hb * 4 + q + 1) * 128],
                                  identf[:])
            b.on("tensor", tf)
            b.stage(f"tfe{i}_{hb}")
            def tfe(eng, hb=hb):
                for q in range(4):
                    eng.tensor_copy(n2f32[:, (hb * 4 + q) * 128:(hb * 4 + q + 1) * 128],
                                    pS[:, q * 128:(q + 1) * 128])
            b.on("vector", tfe)
        b.stage(f"lg{i}")
        def lg(eng):
            for kc in range(DCH):
                eng.matmul(pT[:, 0:NE], n2f32[:, kc * 128:(kc + 1) * 128], simf32[kc][:],
                           start=(kc == 0), stop=(kc == DCH - 1))
        b.on("tensor", lg)
        def rnorm(eng):
            eng.tensor_mul(ln["sq"][:], n2t[:], n2t[:])
            eng.drain()
            eng.reduce_sum(st1[:], ln["sq"][:], axis=AX.X)
        b.on("vector", rnorm)
        b.stage(f"rn2{i}")
        b.on("scalar", lambda eng: eng.activation(out=st2[:], in_=st1[:], func=AF.Sqrt))
        b.stage(f"rn3{i}")
        def rn3(eng):
            eng.tensor_scalar_max(st2[:], st2[:], 1e-12)
            eng.drain()
            eng.reciprocal(rn[:], st2[:])
            eng.drain()
            eng.tensor_mul(lgw[:], pT[:, 0:NE], rivbc[:])
            eng.drain()
            eng.tensor_scalar_mul(lgw[:], lgw[:], rn[:])
            eng.drain()
            eng.tensor_sub(lgw[:], lgw[:], sgbc[:])
        b.on("vector", rn3)
        b.stage(f"gw{i}")
        def gw(eng):
            eng.tensor_scalar_max(gtd[:], lgw[:], 0.0)
            eng.reduce_max(st1[:], lgw[:], axis=AX.X)
            eng.drain()
            eng.tensor_scalar(out=msk[:], in0=lgw[:], scalar1=st1[:], scalar2=None,
                              op0=ALU.is_ge, op1=ALU.bypass)
            eng.drain()
            eng.scalar_tensor_tensor(out=msk[:], in0=msk[:], scalar=NEG, in1=lgw[:],
                                     op0=ALU.mult, op1=ALU.add)
            eng.drain()
            eng.reduce_max(st2[:], msk[:], axis=AX.X)
            eng.drain()
            eng.tensor_scalar(out=msk[:], in0=lgw[:], scalar1=st2[:], scalar2=None,
                              op0=ALU.is_ge, op1=ALU.bypass)
            eng.tensor_scalar(out=wexp[:], in0=lgw[:], scalar1=0.0, scalar2=None,
                              op0=ALU.is_gt, op1=ALU.bypass)
            eng.tensor_scalar(out=st3[:], in0=st1[:], scalar1=0.0, scalar2=None,
                              op0=ALU.is_gt, op1=ALU.bypass)
            eng.drain()
            eng.tensor_sub(wexp[:], wexp[:], msk[:])
            eng.drain()
            eng.tensor_scalar_mul(wexp[:], wexp[:], st3[:])
            eng.drain()
            eng.tensor_add(msk[:], msk[:], wexp[:])
            eng.tensor_scalar_max(st1[:], st1[:], 0.0)
            eng.drain()
            eng.tensor_scalar_mul(st1[:], st1[:], -1.0)
        b.on("vector", gw)
        b.stage(f"gw2{i}")
        b.on("scalar", lambda eng: eng.activation(out=wexp[:], in_=gtd[:], func=AF.Exp,
                                                  bias=st1[:]))
        b.stage(f"gw3{i}")
        def gw3(eng):
            eng.tensor_mul(wexp[:], wexp[:], msk[:])
            eng.drain()
            eng.reduce_sum(st2[:], wexp[:], axis=AX.X)
            eng.drain()
            eng.reciprocal(st3[:], st2[:])
            eng.drain()
            eng.tensor_scalar_mul(wexp[:], wexp[:], st3[:])
        b.on("vector", gw3)
        b.stage(f"gs{i}")
        b.dma("sync", rw2_out[i * P:(i + 1) * P, :], wexp[:])
    b.finalize()
    return nc


def host_l8b_inmaps(inp, l6res, l7b, l8a):
    import ml_dtypes
    oall = _f32(np.asarray(inp["L1_ffn_o_proj"]).transpose(1, 0, 2).reshape(HH, NE * D))
    ones = np.ones((1, P), dtype=np.float32)
    identb = np.eye(P, dtype=ml_dtypes.bfloat16)
    jl = np.arange(P)
    il = np.arange(1024)
    mdiag = np.zeros((8, P, 1024), dtype=ml_dtypes.bfloat16)
    for dix in range(8):
        jg = dix * P + jl
        mdiag[dix] = np.where(jg[:, None] > il[None, :], NEG, 0.0)
    mdiag = mdiag.reshape(8 * P, 1024)
    maps = []
    for c in range(NCORES):
        bb, hh = c // 2, c % 2
        peer = 2 * bb + (1 - hh)
        kfull = np.concatenate([_f32(l8a[c]["kmix"]), _f32(l8a[peer]["kmix"])], axis=0)
        vfull = np.concatenate([_f32(l8a[c]["vmix"]), _f32(l8a[peer]["vmix"])], axis=0)
        va = np.ones((2 * T, HH + 1), dtype=ml_dtypes.bfloat16)
        va[:, 0:HH] = vfull
        kn2 = np.array([[l8a[c]["kn8"][0, 0], l8a[peer]["kn8"][0, 0]]], np.float32)
        pb = np.array([[0.0 if hh == 1 else NEG]], np.float32)
        maps.append({"qmix": _f32(l8a[c]["qmix"]), "kfull": kfull, "vfull": va,
                     "rw": _f32(l8a[c]["rw"]), "o_all": oall,
                     "x1": _f32(l6res[c]["xn1"]),
                     "ln2_w": _f32(inp["L1_ffn_ln2_w"].reshape(1, D)),
                     "ln2_b": _f32(inp["L1_ffn_ln2_b"].reshape(1, D)),
                     "moe_sim": _f32(inp["L1_ffn_moe_sim"]),
                     "moe_gates": _f32(inp["L1_ffn_moe_gates"].reshape(1, NE)),
                     "qn8": _f32(l8a[c]["qn8"]), "kn8": kn2, "peer_bias": pb,
                     "mask_diag": mdiag, "ones_col": ones, "ident_bf": identb})
    return maps


def run_launch8b(inp, l6res, l7b, l8a):
    nc = build_launch8b()
    res = run_bass_kernel_spmd(nc, host_l8b_inmaps(inp, l6res, l7b, l8a),
                               core_ids=list(range(NCORES)))
    return res.results



def build_launch9():
    nc = bass.Bass("TRN2", target_bir_lowering=False, detect_race_conditions=False)
    n2_in = _din(nc, "n2", [T, D])
    rw2_in = _din(nc, "rw2", [T, NE])
    hs_in = _din(nc, "hs", [T, D])
    s1_in = _din(nc, "s1", [T, D])
    x1_in = _din(nc, "x1", [T, D])
    w1_in = _din(nc, "w1cat", [NE * D, D])
    w2_in = _din(nc, "w2cat", [NE * D, D])
    onesc = _din(nc, "ones_col", [1, P])
    identb_in = _din(nc, "ident_bf", [P, P], BF16)
    x2_out = _dout(nc, "x2", [T, D])

    b = Builder(nc)
    n2fm = [b.sb(f"n2fm{c}", [P, T], BF16) for c in range(DCH)]
    hfm = [b.sb(f"hfm{c}", [P, T], BF16) for c in range(DCH)]
    w1e = [b.sb(f"w1e{c}", [P, D], BF16) for c in range(DCH)]
    w2e = [b.sb(f"w2e{c}", [P, D], BF16) for c in range(DCH)]
    acc = [b.sb(f"acc{i}", [P, D], F32) for i in range(NTT)]
    rwa = b.sb("rwa", [P, NE * NTT], F32)
    n2t = b.sb("n2t", [P, D], F32)
    n2b = b.sb("n2b", [P, D], BF16)
    hst = b.sb("hst", [P, D], F32)
    s1t = b.sb("s1t", [P, D], F32)
    x1t = b.sb("x1t", [P, D], F32)
    gt = b.sb("gt", [P, D], F32)
    ones_sb = b.sb("onesc", [1, P], F32)
    identb = b.sb("idb", [P, P], BF16)
    pA = b.ps("pA", [P, 1024], F32)
    pB = b.ps("pB", [P, 1024], F32)
    pT = b.ps("pT", [P, 512], F32)
    pTb = pT[:].bitcast(BF16)

    b.stage("load")
    b.dma("sync", ones_sb[:], onesc)
    b.dma("sync", identb[:], identb_in)
    for i in range(NTT):
        b.dma("sync", rwa[:, i * NE:(i + 1) * NE], rw2_in[i * P:(i + 1) * P, :])
    for i in range(NTT):
        b.stage(f"nl{i}")
        b.dma("sync", n2t[:], n2_in[i * P:(i + 1) * P, :])
        b.stage(f"nc{i}")
        b.on("vector", lambda eng: eng.tensor_copy(n2b[:], n2t[:]))
        b.stage(f"ntp{i}")
        def tr(eng):
            for dc in range(DCH):
                pp = pTb if dc < 4 else pA[:].bitcast(BF16)
                eng.transpose(pp[:, (dc % 4) * 128:(dc % 4 + 1) * 128],
                              n2b[:, dc * 128:(dc + 1) * 128], identb[:])
        b.on("tensor", tr)
        b.stage(f"ne{i}")
        def ev(eng, i=i):
            for dc in range(DCH):
                pp = pTb if dc < 4 else pA[:].bitcast(BF16)
                eng.tensor_copy(n2fm[dc][:, i * P:(i + 1) * P],
                                pp[:, (dc % 4) * 128:(dc % 4 + 1) * 128])
        b.on("vector", ev)
    b.stage("acc0")
    def a0(eng):
        for i in range(NTT):
            eng.memset(acc[i][:], 0.0)
    b.on("vector", a0)
    for e in range(NE):
        b.stage(f"ew{e}")
        for c in range(DCH):
            b.dma("gpsimd", w1e[c][:], w1_in[e * D + c * P:e * D + (c + 1) * P, :])
            b.dma("gpsimd", w2e[c][:], w2_in[e * D + c * P:e * D + (c + 1) * P, :])
        for oc in range(DCH):
            for mg in range(5):
                b.stage(f"e{e}h{oc}_{mg}")
                if mg < 4:
                    pp = pA if mg % 2 == 0 else pB
                    def hm(eng, oc=oc, mg=mg, pp=pp):
                        for kc in range(DCH):
                            eng.matmul(pp[:, 0:512], w1e[kc][:, oc * P:(oc + 1) * P],
                                       n2fm[kc][:, mg * 512:(mg + 1) * 512],
                                       start=(kc == 0), stop=(kc == DCH - 1))
                    b.on("tensor", hm)
                if mg > 0:
                    pq_ = pA if (mg - 1) % 2 == 0 else pB
                    def he(eng, oc=oc, mg=mg, pq_=pq_):
                        eng.activation(out=hfm[oc][:, (mg - 1) * 512:mg * 512],
                                       in_=pq_[:, 0:512], func=AF.Gelu)
                    b.on("scalar", he)
        for i in range(NTT):
            b.stage(f"e{e}y{i}")
            def ym(eng, e=e, i=i):
                for ng in range(2):
                    for kc in range(DCH):
                        eng.matmul(pA[:, ng * 512:(ng + 1) * 512],
                                   hfm[kc][:, i * P:(i + 1) * P],
                                   w2e[kc][:, ng * 512:(ng + 1) * 512],
                                   start=(kc == 0), stop=(kc == DCH - 1))
            b.on("tensor", ym)
            b.stage(f"e{e}x{i}")
            def yx(eng, e=e, i=i):
                eng.scalar_tensor_tensor(out=acc[i][:], in0=pA[:, 0:1024],
                                         scalar=rwa[:, i * NE + e:i * NE + e + 1],
                                         in1=acc[i][:], op0=ALU.mult, op1=ALU.add)
            b.on("vector", yx)
    for i in range(NTT):
        b.stage(f"fl{i}")
        b.dma("sync", hst[:], hs_in[i * P:(i + 1) * P, :])
        b.dma("sync", s1t[:], s1_in[i * P:(i + 1) * P, :])
        b.dma("sync", x1t[:], x1_in[i * P:(i + 1) * P, :])
        b.stage(f"fg{i}")
        def fg(eng, i=i):
            eng.tensor_add(gt[:], hst[:], acc[i][:])
        b.on("vector", fg)
        b.stage(f"fs{i}")
        b.on("scalar", lambda eng: eng.activation(out=gt[:], in_=gt[:], func=AF.Silu))
        b.stage(f"fc{i}")
        def fc(eng):
            eng.tensor_mul(gt[:], gt[:], s1t[:])
            eng.drain()
            eng.tensor_add(gt[:], gt[:], x1t[:])
        b.on("vector", fc)
        b.stage(f"fo{i}")
        b.dma("sync", x2_out[i * P:(i + 1) * P, :], gt[:])
    b.finalize()
    return nc


def host_l9_inmaps(inp, l6res, l7b, l8b):
    import ml_dtypes
    w1cat = _f32(np.asarray(inp["L1_ffn_w1"]).reshape(NE * D, D))
    w2cat = _f32(np.asarray(inp["L1_ffn_w2"]).reshape(NE * D, D))
    ones = np.ones((1, P), dtype=np.float32)
    identb = np.eye(P, dtype=ml_dtypes.bfloat16)
    maps = []
    for c in range(NCORES):
        maps.append({"n2": _f32(l8b[c]["n2"]), "rw2": _f32(l8b[c]["rw2"]),
                     "hs": _f32(l8b[c]["hs"]), "s1": _f32(l7b[c]["s1"]),
                     "x1": _f32(l6res[c]["x1"]), "w1cat": w1cat, "w2cat": w2cat,
                     "ones_col": ones, "ident_bf": identb})
    return maps


def run_launch9(inp, l6res, l7b, l8b):
    nc = build_launch9()
    res = run_bass_kernel_spmd(nc, host_l9_inmaps(inp, l6res, l7b, l8b),
                               core_ids=list(range(NCORES)))
    return res.results



def kernel(**inputs):
    inp = {k: np.asarray(v) for k, v in inputs.items()}
    x = _f32(inp["L0_attn_hidden_states"])

    l1 = [dict(r) for r in run_launch1(inp)]
    l2 = [dict(r) for r in run_launch2(inp, l1)]
    l3 = [dict(r) for r in run_launch3(inp, l1, l2)]
    xn_full = assemble(l1, "xn")
    f1 = [dict(r) for r in run_ffn0(inp, 0, xn_full)]
    x1f = assemble(f1, "xout")
    f2 = [dict(r) for r in run_ffn0(inp, 1, x1f)]
    x_own = [np.ascontiguousarray(x[c // 2, (c % 2) * T:(c % 2 + 1) * T])
             for c in range(NCORES)]
    l6 = [dict(r) for r in run_launch6(x_own, l3, f2)]
    l7a = [dict(r) for r in run_launch7a(inp, l6)]
    l7b = [dict(r) for r in run_launch7b(inp, l7a)]
    l8a = [dict(r) for r in run_launch8a(inp, l6)]
    l8b = [dict(r) for r in run_launch8b(inp, l6, l7b, l8a)]
    l9 = [dict(r) for r in run_launch9(inp, l6, l7b, l8b)]
    out = assemble(l9, "x2")
    return out.astype(np.float32)
